# version 14
# baseline (speedup 1.0000x reference)
import dataclasses
import numpy as np
import ml_dtypes
import concourse.bass as bass
import concourse.mybir as mybir
from concourse.bass_utils import run_bass_kernel_spmd

F32 = mybir.dt.float32
BF16 = mybir.dt.bfloat16
AF = mybir.ActivationFunctionType
ALU = mybir.AluOpType

B, S, D = 8, 4096, 1024
T = B * S
LCH = 16
NR = LCH * B
N_CH = S // LCH
NCORES = 8
FSL = 128
E = 16
EPS = 1e-10
LN_EPS = 1e-5
CSC = 2.0 / (B * D)

OFF_ABC = 0
OFF_LOG = 256
OFF_APP = 0
OFF_T2 = 128
OFF_U = 256
LAG = 10


class Sem:
    def __init__(self, h, key, cnt):
        self.h, self.key, self.cnt = h, key, cnt


class MIns:
    __slots__ = ("ins",)
    def __init__(self, ins):
        self.ins = ins
    def then_inc(self, sem, n):
        sem.cnt[sem.key] += n
        if self.ins is not None:
            self.ins.then_inc(sem.h, n)
        return self


class MEng:
    def __init__(self, eng):
        self.eng = eng
        self.last = None
    def wait(self, sem):
        if self.eng is not None:
            self.eng.wait_ge(sem.h, sem.cnt[sem.key])
    def wait_at(self, sem, val):
        if self.eng is not None:
            self.eng.wait_ge(sem.h, val)
    def gap(self):
        if self.eng is not None:
            self.eng.nop(nofuse=True)
    def seal(self, sem, n=1):
        if self.last is None:
            mi = MIns(self.eng.nop() if self.eng is not None else None)
        else:
            mi = self.last
        mi.then_inc(sem, n)
        self.last = None
    def __getattr__(self, name):
        def call(*a, **k):
            if self.eng is None:
                mi = MIns(None)
            else:
                mi = MIns(getattr(self.eng, name)(*a, **k))
            self.last = mi
            return mi
        return call


def build_program(nch=N_CH):
    nc = bass.Bass("TRN2", target_bir_lowering=False)

    xr_h = nc.dram_tensor("xr", [T, D], F32, kind="ExternalInput").ap()
    xf_h = nc.dram_tensor("xf", [N_CH, 8, 128, 128], F32, kind="ExternalInput").ap()
    xres_h = nc.dram_tensor("xres", [T, FSL], F32, kind="ExternalInput").ap()
    w0t_h = nc.dram_tensor("w0t", [8, 128, 128], F32, kind="ExternalInput").ap()
    wvt_h = nc.dram_tensor("wvt", [8, 128, 128], F32, kind="ExternalInput").ap()
    bkvr_h = nc.dram_tensor("bkvr", [1, 128], F32, kind="ExternalInput").ap()
    eta_h = nc.dram_tensor("eta", [1, 1], F32, kind="ExternalInput").ap()
    rwt_h = nc.dram_tensor("rwt", [8, 128, 16], F32, kind="ExternalInput").ap()
    rbr_h = nc.dram_tensor("rbr", [1, 16], F32, kind="ExternalInput").ap()
    wet_h = nc.dram_tensor("wet", [8, 128, 2048], F32, kind="ExternalInput").ap()
    idn_h = nc.dram_tensor("idn", [128, 128], BF16, kind="ExternalInput").ap()
    maskl_h = nc.dram_tensor("maskl", [128, 128], BF16, kind="ExternalInput").ap()
    masku_h = nc.dram_tensor("masku", [128, 128], BF16, kind="ExternalInput").ap()
    onesr_h = nc.dram_tensor("onesr", [1, 128], BF16, kind="ExternalInput").ap()
    negonesr_h = nc.dram_tensor("negonesr", [1, 128], BF16, kind="ExternalInput").ap()
    onesc_h = nc.dram_tensor("onesc", [128, 1], BF16, kind="ExternalInput").ap()
    out_h = nc.dram_tensor("out", [T, FSL], F32, kind="ExternalOutput").ap()
    csum_h = nc.dram_tensor("csum_scratch", [1, 16, 128], F32, kind="Internal").ap()

    sb = nc.alloc_sbuf_tensor
    xr_r = [sb(f"s_xr{i}", [128, 1024], F32).ap() for i in range(2)]
    xf_r = [sb(f"s_xf{i}", [128, 1024], F32).ap() for i in range(2)]
    xres_r = [sb(f"s_xres{i}", [128, 128], F32).ap() for i in range(13)]
    xfb_r = [sb(f"s_xfb{i}", [128, 1024], BF16).ap() for i in range(12)]
    xrbs_r = [sb(f"s_xrbs{i}", [128, 1024], BF16).ap() for i in range(2)]
    Wf = sb("s_Wf", [128, 1024], F32).ap()
    Wb = sb("s_Wb", [128, 1024], BF16).ap()
    wvf = sb("s_wvf", [128, 1024], F32).ap()
    wvb = sb("s_wvb", [128, 1024], BF16).ap()
    rwf = sb("s_rwf", [128, 128], F32).ap()
    rwb = sb("s_rwb", [128, 128], BF16).ap()
    wstg = sb("s_wstg", [128, 4096], F32).ap()
    web = sb("s_web", [128, 16384], BF16).ap()
    idn = sb("s_idn", [128, 128], BF16).ap()
    maskl = sb("s_maskl", [128, 128], BF16).ap()
    masku = sb("s_masku", [128, 128], BF16).ap()
    onesr = sb("s_onesr", [1, 128], BF16).ap()
    negonesr = sb("s_negonesr", [1, 128], BF16).ap()
    onesc = sb("s_onesc", [128, 1], BF16).ap()
    bkvr_f = sb("s_bkvrf", [1, 128], F32).ap()
    bkvr = sb("s_bkvr", [1, 128], BF16).ap()
    rbr_f = sb("s_rbrf", [1, 16], F32).ap()
    rbr = sb("s_rbr", [1, 16], BF16).ap()
    eta_bc = sb("s_eta", [128, 1], F32).ap()
    etan_bc = sb("s_etan", [128, 1], F32).ap()
    wvcol = sb("s_wvcol", [1, 128], BF16).ap()
    wcol0 = sb("s_wcol0", [1, 128], BF16).ap()
    rwsum = sb("s_rwsum", [1, 16], BF16).ap()
    csum_f = sb("s_csumf", [1, 2048], F32).ap()
    csum16 = sb("s_csum16", [16, 128], F32).ap()
    csb = sb("s_csb", [16, 128], BF16).ap()
    bnrec = sb("s_bnrec", [128, 12], F32).ap()
    mv = sb("s_mv", [128, 2], F32).ap()
    t0c = sb("s_t0c", [128, 1], F32).ap()
    lnc = sb("s_lnc", [128, 1], F32).ap()
    a8_r = [sb(f"s_a8{i}", [128, 8], F32).ap() for i in range(2)]
    ra_col = sb("s_racol", [128, 1], F32).ap()
    nca8_r = [sb(f"s_nca8{i}", [128, 8], F32).ap() for i in range(2)]
    negm8_r = [sb(f"s_negm8{i}", [128, 8], F32).ap() for i in range(2)]
    stats5_r = [sb(f"s_stats5{i}", [128, 5], BF16).ap() for i in range(2)]
    m1024c = sb("s_m1024c", [128, 1], F32).ap()
    mnca_r = [sb(f"s_mncab{i}", [128, 1], BF16).ap() for i in range(2)]
    rows_r = [[sb(f"s_row{k}_{i}", [1, 128], BF16).ap() for k in range(5)] for i in range(2)]
    u_row = sb("s_urow", [1, 128], BF16).ap()
    amaskl_r = [sb(f"s_amaskl{i}", [128, 128], BF16).ap() for i in range(2)]
    amasku_r = [sb(f"s_amasku{i}", [128, 128], BF16).ap() for i in range(2)]
    Np_b = [sb(f"s_Np{i}", [128, 128], BF16).ap() for i in range(2)]
    Nt_b = [sb(f"s_Nt{i}", [128, 128], BF16).ap() for i in range(2)]
    Ef = [sb(f"s_Ef{i}", [128, 128], F32).ap() for i in range(2)]
    Eb = [sb(f"s_Eb{i}", [128, 128], BF16).ap() for i in range(2)]
    Vn = sb("s_Vn", [128, 128], F32).ap()
    s_tile_r = [sb(f"s_stile{i}", [128, 128], F32).ap() for i in range(12)]
    eL8_r = [sb(f"s_eL8{i}", [128, 128], F32).ap() for i in range(2)]
    se8_r = [sb(f"s_se8{i}", [128, 8], F32).ap() for i in range(2)]
    epsZ8 = sb("s_epsZ8", [128, 8], F32).ap()
    m88 = sb("s_m88", [128, 64], F32).ap()
    S68 = sb("s_S68", [128, 8], F32).ap()
    den8 = sb("s_den8", [128, 8], F32).ap()
    rS28 = sb("s_rS28", [128, 8], F32).ap()
    rden8 = sb("s_rden8", [128, 8], F32).ap()
    sig8 = sb("s_sig8", [128, 8], F32).ap()
    epssig8_r = [sb(f"s_epssig8{i}", [128, 8], F32).ap() for i in range(3)]
    maskt8 = sb("s_maskt8", [128, 128], F32).ap()
    graw8 = sb("s_graw8", [128, 128], F32).ap()
    gateA8 = sb("s_gateA8", [128, 128], F32).ap()
    gate8_r = [sb(f"s_gate8{i}", [128, 128], F32).ap() for i in range(3)]
    gm8_r = [sb(f"s_gm8{i}", [128, 128], BF16).ap() for i in range(3)]
    gmT_r = [sb(f"s_gmT{i}", [16, 128], BF16).ap() for i in range(12)]
    tmp16 = sb("s_tmp16", [128, 2048], BF16).ap()
    tsum = sb("s_tsum", [128, 1024], F32).ap()
    moe = sb("s_moe", [128, 128], F32).ap()
    moe3 = sb("s_moe3", [128, 128], F32).ap()
    eSg = sb("s_eSg", [128, 128], F32).ap()
    h_t = sb("s_ht", [128, 128], F32).ap()
    lnd = sb("s_lnd", [128, 1024 // 8], F32).ap()
    rdsg = sb("s_rdsg", [128, 128], F32).ap()
    hs = sb("s_hs", [128, 128], F32).ap()
    out_t = sb("s_outt", [128, 128], F32).ap()

    pgram = nc.alloc_psum_tensor("p_gram", [128, 128], F32).ap()
    pxw = nc.alloc_psum_tensor("p_xw", [128, 128], F32).ap()
    pv = nc.alloc_psum_tensor("p_v", [128, 128], F32).ap()
    pe = nc.alloc_psum_tensor("p_e", [128, 1024], F32).ap()
    scr = nc.alloc_psum_tensor("p_scr", [128, 512], F32).ap()
    pbt = nc.alloc_psum_tensor("p_bt", [128, 1024], BF16).ap()
    papp = nc.alloc_psum_tensor("p_app", [128, 512], F32).ap()

    cnt = {}
    semD = Sem(nc.alloc_semaphore(name="semD"), "D", cnt)
    semD0 = Sem(nc.alloc_semaphore(name="semD0"), "D0", cnt)
    semD1 = Sem(nc.alloc_semaphore(name="semD1"), "D1", cnt)
    semDr = [semD0, semD1]
    semT = Sem(nc.alloc_semaphore(name="semT"), "T", cnt)
    semV = Sem(nc.alloc_semaphore(name="semV"), "V", cnt)
    semS = Sem(nc.alloc_semaphore(name="semS"), "S", cnt)
    semG = Sem(nc.alloc_semaphore(name="semG"), "G", cnt)
    semO = Sem(nc.alloc_semaphore(name="semO"), "O", cnt)

    def emit(active):
        for k in ("D", "D0", "D1", "T", "V", "S", "G", "O"):
            cnt[k] = 0
        sy = MEng(active.get("sync"))
        t = MEng(active.get("tensor"))
        v = MEng(active.get("vector"))
        s = MEng(active.get("scalar"))
        g = MEng(active.get("gpsimd"))

        sy.dma_start(out=idn, in_=idn_h).then_inc(semD, 16)
        sy.dma_start(out=maskl, in_=maskl_h).then_inc(semD, 16)
        sy.dma_start(out=masku, in_=masku_h).then_inc(semD, 16)
        sy.dma_start(out=onesr, in_=onesr_h).then_inc(semD, 16)
        sy.dma_start(out=negonesr, in_=negonesr_h).then_inc(semD, 16)
        sy.dma_start(out=onesc, in_=onesc_h).then_inc(semD, 16)
        sy.dma_start(out=bkvr_f, in_=bkvr_h).then_inc(semD, 16)
        sy.dma_start(out=rbr_f, in_=rbr_h).then_inc(semD, 16)
        sy.dma_start(out=eta_bc, in_=eta_h.to_broadcast((128, 1))).then_inc(semD, 16)
        for db in range(8):
            sy.dma_start(out=Wf[:, db * 128:(db + 1) * 128], in_=w0t_h[db]).then_inc(semD, 16)
            sy.dma_start(out=wvf[:, db * 128:(db + 1) * 128], in_=wvt_h[db]).then_inc(semD, 16)
            sy.dma_start(out=rwf[:, db * 16:(db + 1) * 16], in_=rwt_h[db]).then_inc(semD, 16)
        for q in range(4):
            if q > 0:
                sy.wait(semV)
            sy.dma_start(out=wstg[:, 0:2048], in_=wet_h[2 * q]).then_inc(semD, 16)
            sy.dma_start(out=wstg[:, 2048:4096], in_=wet_h[2 * q + 1]).then_inc(semD, 16)
            v.wait(semD)
            v.tensor_copy(out=web[:, q * 4096:(q + 1) * 4096], in_=wstg[:, :])
            v.seal(semV)
        v.wait(semD)
        v.tensor_copy(out=Wb, in_=Wf)
        v.tensor_copy(out=wvb, in_=wvf)
        v.tensor_copy(out=rwb, in_=rwf)
        v.tensor_copy(out=bkvr, in_=bkvr_f)
        v.tensor_copy(out=rbr, in_=rbr_f)
        v.tensor_scalar(out=etan_bc, in0=eta_bc, scalar1=-CSC, scalar2=None, op0=ALU.mult)
        v.seal(semV)
        t.wait(semV)
        for db in range(8):
            t.matmul(scr[:1, 0:128], onesc, Wb[:, db * 128:(db + 1) * 128],
                     start=(db == 0), stop=(db == 7))
        for db in range(8):
            t.matmul(scr[:1, 128:256], onesc, wvb[:, db * 128:(db + 1) * 128],
                     start=(db == 0), stop=(db == 7))
        for db in range(8):
            t.matmul(scr[:1, 256:272], onesc, rwb[:, db * 16:(db + 1) * 16],
                     start=(db == 0), stop=(db == 7))
        t.seal(semT)
        v.wait(semT)
        v.tensor_copy(out=wcol0, in_=scr[0:1, 0:128])
        v.tensor_copy(out=wvcol, in_=scr[0:1, 128:256])
        v.tensor_copy(out=rwsum, in_=scr[0:1, 256:272])
        v.seal(semV)
        for q in range(4):
            t.wait(semV)
            for db in range(8):
                t.matmul(scr[:1, 0:512], onesc, web[:, db * 2048 + q * 512: db * 2048 + (q + 1) * 512],
                         start=(db == 0), stop=(db == 7))
            t.seal(semT)
            v.wait(semT)
            v.tensor_copy(out=csum_f[:, q * 512:(q + 1) * 512], in_=scr[0:1, 0:512])
            v.seal(semV)
        sy.wait(semV)
        sy.dma_start(out=csum_h, in_=csum_f[:, :].rearrange("o (e f) -> o e f", e=16)).then_inc(semD, 16)
        sy.wait(semD)
        sy.dma_start(out=csum16, in_=csum_h[0]).then_inc(semD, 16)
        v.wait(semD)
        v.tensor_copy(out=csb, in_=csum16)
        v.seal(semV)
        t.wait(semV)

        R = {}

        def seal_r(eng, sem, name, c):
            eng.seal(sem)
            R[(name, c)] = cnt[sem.key]

        def bc816(ap2d):
            return dataclasses.replace(ap2d, ap=type(ap2d.ap)([[8, 128], [1, 8], [0, 16]]))

        R[("Wb", -1)] = cnt["V"]
        R[("WbS0", -1)] = cnt["S"]
        R[("WbS1", -1)] = cnt["S"]

        def issue_loads(c):
            if c >= nch:
                return
            sl = c % 2
            sd = semDr[sl]
            if c >= 2:
                sy.wait_at(semV, R[("bnsA", c - 2)])
                sy.wait_at(semS, R[("sexp2", c - 2)])
                sy.wait_at(semG, R[("xfb", c - 2)])
            if c >= 13:
                sy.wait_at(semV, R[("gout", c - 13)])
            sy.dma_start(out=xr_r[sl], in_=xr_h[c * 128:(c + 1) * 128, :]).then_inc(sd, 16)
            sy.dma_start(out=xf_r[sl][:, :].rearrange("p (a b) -> p a b", a=8),
                         in_=xf_h[c].transpose([1, 0, 2])).then_inc(sd, 16)
            sy.dma_start(out=xres_r[c % 13], in_=xres_h[c * 128:(c + 1) * 128, :]).then_inc(sd, 16)
            dmaD[c] = cnt[sd.key]

        def stats_prep(c):
            stats_a(c)
            stats_b(c)
            stats_c1(c)
            stats_s1a(c)
            stats_s1b(c)
            stats_s2(c)

        def stats_a(c):
            if c >= nch:
                return
            sl = c % 2
            v.wait_at(semDr[c % 2], dmaD[c])
            v.bn_stats(out=bnrec[:, 0:6], in_=xr_r[sl][:, 0:512])
            v.bn_stats(out=bnrec[:, 6:12], in_=xr_r[sl][:, 512:1024])
            seal_r(v, semV, "bnsA", c)

        def stats_b(c):
            if c >= nch:
                return
            v.wait_at(semV, R[("bnsA", c)])
            if c >= 1:
                v.wait_at(semS, R[("sexp2", c - 1)])
            v.bn_aggr(out=mv, in_=bnrec[:, :].rearrange("p (n r) -> p n r", r=3))
            seal_r(v, semV, "bnagg", c)

        def stats_c1(c):
            if c >= nch:
                return
            b2 = (c // 8) % 2
            c8 = c % 8
            v.wait_at(semV, R[("bnagg", c)])
            if c >= 1:
                v.wait_at(semS, R[("sexp", c - 1)])
            v.tensor_scalar(out=t0c, in0=mv[:, 1:2], scalar1=LN_EPS, scalar2=None, op0=ALU.add)
            v.tensor_scalar(out=negm8_r[b2][:, c8:c8 + 1], in0=mv[:, 0:1], scalar1=-1.0,
                            scalar2=None, op0=ALU.mult)
            v.tensor_scalar(out=m1024c, in0=mv[:, 0:1], scalar1=-1024.0, scalar2=None, op0=ALU.mult)
            seal_r(v, semV, "bnT", c)

        def stats_s1a(c):
            if c >= nch:
                return
            b2 = (c // 8) % 2
            c8 = c % 8
            s.wait_at(semV, R[("bnT", c)])
            if c >= 2:
                s.wait_at(semV, R[("urow", c - 2)])
            if c8 == 0 and c // 8 >= 2:
                s.wait_at(semV, R[("g8", c // 8 - 2)])
            s.activation(out=lnc, in_=t0c, func=AF.Ln, bias=0.0, scale=1.0)
            s.drain()
            s.activation(out=a8_r[b2][:, c8:c8 + 1], in_=lnc, func=AF.Exp, bias=0.0, scale=-0.5)
            s.activation(out=ra_col, in_=lnc, func=AF.Exp, bias=0.0, scale=0.5)
            seal_r(s, semS, "sexpA", c)

        def stats_s1b(c):
            if c >= nch:
                return
            sl = c % 2
            b2 = (c // 8) % 2
            c8 = c % 8
            s.wait_at(semS, R[("sexpA", c)])
            if c >= 2:
                s.wait_at(semT, R[("transp", c - 2)])
            s.activation(out=stats5_r[sl][:, 0:1], in_=mv[:, 0:1], func=AF.Identity,
                         bias=0.0, scale=1.0)
            s.activation(out=stats5_r[sl][:, 1:2], in_=a8_r[b2][:, c8:c8 + 1], func=AF.Identity,
                         bias=0.0, scale=1.0)
            s.activation(out=stats5_r[sl][:, 2:3], in_=ra_col, func=AF.Identity,
                         bias=0.0, scale=1.0)
            s.activation(out=stats5_r[sl][:, 3:4], in_=m1024c, func=AF.Identity,
                         bias=0.0, scale=1.0)
            s.activation(out=stats5_r[sl][:, 4:5], in_=negm8_r[b2][:, c8:c8 + 1], func=AF.Identity,
                         bias=0.0, scale=1.0)
            seal_r(s, semS, "sexp", c)

        def stats_s2(c):
            if c >= nch:
                return
            sl = c % 2
            b2 = (c // 8) % 2
            c8 = c % 8
            s.wait_at(semS, R[("sexp", c)])
            if c >= 2:
                s.wait_at(semT, R[("gW", c - 2)])
            s.activation(out=nca8_r[b2][:, c8:c8 + 1], in_=a8_r[b2][:, c8:c8 + 1],
                         func=AF.Identity, bias=0.0, scale=etan_bc)
            s.drain()
            s.activation(out=mnca_r[sl], in_=nca8_r[b2][:, c8:c8 + 1], func=AF.Identity,
                         bias=0.0, scale=mv[:, 0:1])
            s.activation(out=xrbs_r[sl], in_=xr_r[sl], func=AF.Identity,
                         bias=0.0, scale=nca8_r[b2][:, c8:c8 + 1])
            seal_r(s, semS, "sexp2", c)

        dmaD = {}
        issue_loads(0)
        g.wait_at(semDr[0], dmaD[0])
        g.tensor_copy(out=xfb_r[0], in_=xf_r[0])
        seal_r(g, semG, "xfb", 0)
        stats_prep(0)
        t.wait_at(semG, R[("xfb", 0)])
        for db in range(8):
            xb0 = xfb_r[0][:, db * 128:(db + 1) * 128]
            t.matmul(pgram, xb0, xb0, start=(db == 0), stop=False)
        for db in range(8):
            xb0 = xfb_r[0][:, db * 128:(db + 1) * 128]
            t.matmul(pv, xb0, wvb[:, db * 128:(db + 1) * 128], start=(db == 0), stop=False)
        for db in range(8):
            xb0 = xfb_r[0][:, db * 128:(db + 1) * 128]
            t.matmul(scr[:, OFF_LOG:OFF_LOG + 16], xb0, rwb[:, db * 16:(db + 1) * 16],
                     start=(db == 0), stop=False)
        seal_r(t, semT, "sweepbig", 0)
        t.wait_at(semS, R[("sexp", 0)])
        for k in range(5):
            t.transpose(pbt[0:1, k * 128:(k + 1) * 128], stats5_r[0][:, k:k + 1], idn)
        seal_r(t, semT, "transp", 0)

        for ch in range(nch + LAG):
            act = ch < nch
            lc = ch - LAG
            sl = ch % 2
            b2 = (ch // 8) % 2
            c8 = ch % 8
            xfb = xfb_r[ch % 12]
            acol = a8_r[b2][:, c8:c8 + 1]
            ncacol = nca8_r[b2][:, c8:c8 + 1]
            if lc >= 0:
                lbk = lc // 8
                lgate = gate8_r[lbk % 3]
                lgcol = lambda e: lgate[:, (lc % 8) * 16 + e:(lc % 8) * 16 + e + 1]

            if act:
                issue_loads(ch + 1)

            if c8 == 0 and ch // 8 >= 1 and (ch // 8 - 1) < nch // 8:
                bk = ch // 8 - 1
                eL8 = eL8_r[bk % 2]
                if bk >= 3:
                    v.wait_at(semT, R[("term2", 8 * (bk - 2) - 1)])
                    v.wait_at(semS, R[("evacBs", 8 * (bk - 2) - 1)])
                v.wait_at(semS, R[("eL", 8 * bk + 7)])
                for k in range(8):
                    v.max(out=m88[:, k * 8:(k + 1) * 8], in_=eL8[:, k * 16:(k + 1) * 16])
                v.tensor_scalar(out=epsZ8, in0=se8_r[bk % 2], scalar1=EPS, scalar2=None, op0=ALU.mult)
                v.drain()
                v.reduce_sum(out=S68, in_=m88[:, :].rearrange("p (k e) -> p k e", k=8)[:, :, 0:6],
                             axis=mybir.AxisListType.X)
                m5 = dataclasses.replace(m88[:, :], offset=m88.offset + 5,
                                         ap=type(m88.ap)([[64, 128], [8, 8], [0, 16]]))
                v.tensor_tensor(out=maskt8, in0=eL8[:, :].rearrange("p (k e) -> p k e", k=8),
                                in1=m5, op=ALU.is_ge)
                v.drain()
                v.reciprocal(out=rS28, in_=S68)
                v.tensor_tensor(out=den8, in0=S68, in1=epsZ8, op=ALU.add)
                v.tensor_tensor(out=graw8, in0=maskt8, in1=eL8, op=ALU.mult)
                v.drain()
                v.tensor_tensor(out=gateA8, in0=graw8[:, :].rearrange("p (k e) -> p k e", k=8),
                                in1=bc816(rS28[:, :]), op=ALU.mult)
                v.reciprocal(out=rden8, in_=den8)
                v.drain()
                v.tensor_tensor(out=gate8_r[bk % 3],
                                in0=gateA8[:, :].rearrange("p (k e) -> p k e", k=8),
                                in1=bc816(a8_r[bk % 2][:, :]), op=ALU.mult)
                v.tensor_tensor(out=sig8, in0=S68, in1=rden8, op=ALU.mult)
                v.drain()
                v.tensor_tensor(out=gm8_r[bk % 3],
                                in0=gate8_r[bk % 3][:, :].rearrange("p (k e) -> p k e", k=8),
                                in1=bc816(negm8_r[bk % 2][:, :]), op=ALU.mult)
                v.tensor_scalar(out=epssig8_r[bk % 3], in0=sig8, scalar1=EPS, scalar2=None, op0=ALU.mult)
                seal_r(v, semV, "g8", bk)

            if act:
                s.wait_at(semT, R[("transp", ch)])
                for k in range(5):
                    s.activation(out=rows_r[sl][k], in_=pbt[0:1, k * 128:(k + 1) * 128],
                                 func=AF.Identity, bias=0.0, scale=1.0)
                seal_r(s, semS, "rows", ch)

            if act and lc >= 1:
                s.wait_at(semV, R[("moe3", lc - 1)])
                s.activation(out=eSg, in_=moe3, func=AF.Exp, bias=0.0, scale=-1.0)
                seal_r(s, semS, "eSg", lc - 1)
                if lc >= 2:
                    s.wait_at(semG, R[("ghs", lc - 2)])
                s.wait_at(semS, R[("eSg", lc - 1)])
                s.activation(out=lnd, in_=eSg, func=AF.Ln, bias=1.0, scale=1.0)
                s.drain()
                s.activation(out=rdsg, in_=lnd, func=AF.Exp, bias=0.0, scale=-1.0)
                seal_r(s, semS, "dsg", lc - 1)

            if act:
                rows = rows_r[sl]
                t.wait_at(semS, R[("rows", ch)])
                t.matmul(scr[:, OFF_LOG:OFF_LOG + 16], rows[4], rwsum, start=False, stop=False)
                t.matmul(scr[:, OFF_LOG:OFF_LOG + 16], rows[2], rbr, start=False, stop=True)
                t.matmul(scr[:, OFF_ABC:OFF_ABC + 128], onesr, rows[1], start=True, stop=True)
                seal_r(t, semT, "abc", ch)
                t.matmul(pgram, rows[3], rows[0], start=False, stop=True)
                t.matmul(pv, rows[4], wvcol, start=False, stop=False)
                t.matmul(pv, rows[2], bkvr, start=False, stop=True)
                if ch >= 1:
                    t.wait_at(semV, R[("H", ch - 1)])
                t.wait_at(semS, R[("WbS0", ch - 1)])
                for db in range(4):
                    xb = xfb[:, db * 128:(db + 1) * 128]
                    t.matmul(pxw, xb, Wb[:, db * 128:(db + 1) * 128], start=(db == 0), stop=False)
                t.wait_at(semS, R[("WbS1", ch - 1)])
                for db in range(4, 8):
                    xb = xfb[:, db * 128:(db + 1) * 128]
                    t.matmul(pxw, xb, Wb[:, db * 128:(db + 1) * 128], start=False, stop=False)
                t.matmul(pxw, rows[4], wcol0, start=False, stop=True)
                seal_r(t, semT, "k1s", ch)
                if lc >= 0:
                    lfb0 = xfb_r[lc % 12]
                    t.wait_at(semV, R[("g8", lc // 8)])
                    for db in range(4):
                        xb = lfb0[:, db * 128:(db + 1) * 128]
                        base = db * 2048
                        t.matmul(pe[:, 0:512], xb, web[:, base:base + 512],
                                 start=(db == 0), stop=False)
                        t.matmul(pe[:, 512:1024], xb, web[:, base + 512:base + 1024],
                                 start=(db == 0), stop=False)

            if act:
                v.wait_at(semT, R[("abc", ch)])
                v.tensor_tensor(out=amaskl_r[sl], in0=scr[:, OFF_ABC:OFF_ABC + 128],
                                in1=maskl, op=ALU.mult)
                v.tensor_tensor(out=amasku_r[sl], in0=scr[:, OFF_ABC:OFF_ABC + 128],
                                in1=masku, op=ALU.mult)
                seal_r(v, semV, "amask", ch)
                v.wait_at(semT, R[("k1s", ch)])
                v.tensor_scalar(out=Vn, in0=pv, scalar1=acol, scalar2=None, op0=ALU.mult)
                seal_r(v, semV, "vn", ch)
                v.wait_at(semV, R[("amask", ch)])
                v.scalar_tensor_tensor(out=Np_b[0], in0=pgram, scalar=ncacol, in1=amaskl_r[sl],
                                       op0=ALU.mult, op1=ALU.mult)
                v.scalar_tensor_tensor(out=Nt_b[0], in0=pgram, scalar=ncacol, in1=amasku_r[sl],
                                       op0=ALU.mult, op1=ALU.mult)
                v.wait_at(semV, R[("vn", ch)])
                v.scalar_tensor_tensor(out=Ef[0], in0=pxw, scalar=acol, in1=Vn,
                                       op0=ALU.mult, op1=ALU.subtract)
                v.scalar_tensor_tensor(out=Eb[0], in0=pxw, scalar=acol, in1=Vn,
                                       op0=ALU.mult, op1=ALU.subtract)
                seal_r(v, semV, "H", ch)

            if act:
                s.wait_at(semT, R[("k1s", ch)])
                s.wait_at(semV, R[("amask", ch)])
                if c8 == 0 and ch // 8 >= 2:
                    s.wait_at(semV, R[("g8", ch // 8 - 2)])
                s.activation(out=eL8_r[b2][:, c8 * 16:(c8 + 1) * 16],
                             in_=scr[:, OFF_LOG:OFF_LOG + 16], func=AF.Exp,
                             bias=0.0, scale=acol, accum_out=se8_r[b2][:, c8:c8 + 1])
                seal_r(s, semS, "eL", ch)

            if act:
                for j in range(4):
                    cur, nxt = j % 2, (j + 1) % 2
                    t.wait_at(semV, R[("H", ch)] if j == 0 else R[(f"stt{j-1}", ch)])
                    t.matmul(papp[:, OFF_APP:OFF_APP + 128], Nt_b[cur], Eb[cur],
                             start=True, stop=True)
                    if j < 3:
                        t.matmul(pgram, Nt_b[cur], Np_b[cur], start=True, stop=True)
                        t.matmul(pv, Np_b[cur], Nt_b[cur], start=True, stop=True)
                    seal_r(t, semT, f"app{j}", ch)
                    if lc >= 0 and j == 0:
                        lfb = xfb_r[lc % 12]
                        for db in range(4, 8):
                            xb = lfb[:, db * 128:(db + 1) * 128]
                            base = db * 2048
                            t.matmul(pe[:, 0:512], xb, web[:, base:base + 512],
                                     start=False, stop=(db == 7))
                            t.matmul(pe[:, 512:1024], xb, web[:, base + 512:base + 1024],
                                     start=False, stop=(db == 7))
                        seal_r(t, semT, "grp0", lc)
                    if lc >= 0 and j == 2:
                        t.wait_at(semS, R[("rows", ch)])
                        t.transpose(pbt[0:16, 640:768],
                                    gm8_r[(lc // 8) % 3][:, (lc % 8) * 16:(lc % 8 + 1) * 16], idn)
                        seal_r(t, semT, "gmT", lc)
                    if lc >= 0 and j in (2, 3):
                        lfb = xfb_r[lc % 12]
                        if j == 2:
                            t.wait_at(semV, R[("evacA", lc)])
                            t.wait_at(semS, R[("evacAs", lc)])
                        for db in range(4 * (j - 2), 4 * (j - 2) + 4):
                            xb = lfb[:, db * 128:(db + 1) * 128]
                            base = db * 2048 + 1024
                            t.matmul(pe[:, 0:512], xb, web[:, base:base + 512],
                                     start=(db == 0), stop=(db == 7))
                            t.matmul(pe[:, 512:1024], xb, web[:, base + 512:base + 1024],
                                     start=(db == 0), stop=(db == 7))
                        if j == 3:
                            seal_r(t, semT, "grp1", lc)
                    v.wait_at(semT, R[(f"app{j}", ch)])
                    v.scalar_tensor_tensor(out=Ef[nxt], in0=papp[:, OFF_APP:OFF_APP + 128],
                                           scalar=1.0, in1=Ef[cur], op0=ALU.mult, op1=ALU.add)
                    v.scalar_tensor_tensor(out=Eb[nxt], in0=papp[:, OFF_APP:OFF_APP + 128],
                                           scalar=1.0, in1=Ef[cur], op0=ALU.mult, op1=ALU.add)
                    if j < 3:
                        v.tensor_copy(out=Np_b[nxt], in_=pgram)
                        v.tensor_copy(out=Nt_b[nxt], in_=pv)
                    if lc >= 0 and j == 2:
                        v.wait_at(semT, R[("gmT", lc)])
                        v.tensor_copy(out=gmT_r[lc % 12], in_=pbt[0:16, 640:768])
                    if lc >= 0 and j == 3:
                        v.wait_at(semT, R[("grp1", lc)])
                        for k in range(4):
                            v.tensor_scalar(out=tmp16[:, k * 128:(k + 1) * 128],
                                            in0=pe[:, k * 128:(k + 1) * 128],
                                            scalar1=lgcol(8 + k), scalar2=None, op0=ALU.mult)
                    seal_r(v, semV, f"stt{j}", ch)
                    if lc >= 1 and j == 0:
                        v.wait_at(semS, R[("dsg", lc - 1)])
                        if lc >= 2:
                            v.wait_at(semG, R[("ghs", lc - 2)])
                        v.tensor_tensor(out=h_t, in0=s_tile_r[(lc - 1) % 12], in1=moe3,
                                        op=ALU.mult)
                        seal_r(v, semV, "hrdsg", lc - 1)
                    if lc >= 0 and j == 1:
                        v.wait_at(semT, R[("grp0", lc)])
                        if lc >= 1:
                            v.wait_at(semV, R[("gpred", lc - 1)])
                        for k in range(4):
                            v.tensor_scalar(out=tmp16[:, (k + 8) * 128:(k + 9) * 128],
                                            in0=pe[:, k * 128:(k + 1) * 128],
                                            scalar1=lgcol(k), scalar2=None, op0=ALU.mult)
                        seal_r(v, semV, "evacA", lc)
                        s.wait_at(semT, R[("grp0", lc)])
                        if lc >= 1:
                            s.wait_at(semV, R[("gpred", lc - 1)])
                        for k in range(4, 8):
                            s.activation(out=tmp16[:, (k + 8) * 128:(k + 9) * 128],
                                         in_=pe[:, k * 128:(k + 1) * 128], func=AF.Identity,
                                         bias=0.0, scale=lgcol(k))
                        seal_r(s, semS, "evacAs", lc)
                    if lc >= 1 and j == 1:
                        g.wait_at(semV, R[("hrdsg", lc - 1)])
                        g.tensor_tensor(out=hs, in0=h_t, in1=rdsg, op=ALU.mult)
                        seal_r(g, semG, "ghs", lc - 1)
                        v.wait_at(semG, R[("ghs", lc - 1)])
                        if lc >= 2:
                            v.wait_at(semO, 16 * (lc - 1))
                        v.tensor_tensor(out=out_t, in0=hs, in1=xres_r[(lc - 1) % 13],
                                        op=ALU.add)
                        seal_r(v, semV, "gout", lc - 1)
                        sy.wait_at(semV, R[("gout", lc - 1)])
                        sy.dma_start(out=out_h[(lc - 1) * 128:lc * 128, :],
                                     in_=out_t).then_inc(semO, 16)
                    if lc >= 0 and j == 2:
                        R[("gmTe_l", lc)] = R[("stt2", ch)]
                    if lc >= 0 and j == 3:
                        R[("evacB", lc)] = R[("stt3", ch)]
                R[("gmTe", ch)] = R[("stt2", ch)]
            else:
                s.wait_at(semV, R[("moe3", lc - 1)])
                s.activation(out=eSg, in_=moe3, func=AF.Exp, bias=0.0, scale=-1.0)
                seal_r(s, semS, "eSg", lc - 1)
                s.wait_at(semG, R[("ghs", lc - 2)])
                s.wait_at(semS, R[("eSg", lc - 1)])
                s.activation(out=lnd, in_=eSg, func=AF.Ln, bias=1.0, scale=1.0)
                s.drain()
                s.activation(out=rdsg, in_=lnd, func=AF.Exp, bias=0.0, scale=-1.0)
                seal_r(s, semS, "dsg", lc - 1)
                v.wait_at(semS, R[("dsg", lc - 1)])
                v.wait_at(semG, R[("ghs", lc - 2)])
                v.tensor_tensor(out=h_t, in0=s_tile_r[(lc - 1) % 12], in1=moe3, op=ALU.mult)
                seal_r(v, semV, "hrdsg", lc - 1)
                g.wait_at(semV, R[("hrdsg", lc - 1)])
                g.tensor_tensor(out=hs, in0=h_t, in1=rdsg, op=ALU.mult)
                seal_r(g, semG, "ghs", lc - 1)
                v.wait_at(semG, R[("ghs", lc - 1)])
                v.wait_at(semO, 16 * (lc - 1))
                v.tensor_tensor(out=out_t, in0=hs, in1=xres_r[(lc - 1) % 13], op=ALU.add)
                seal_r(v, semV, "gout", lc - 1)
                sy.wait_at(semV, R[("gout", lc - 1)])
                sy.dma_start(out=out_h[(lc - 1) * 128:lc * 128, :],
                             in_=out_t).then_inc(semO, 16)
                lfb = xfb_r[lc % 12]
                t.wait_at(semV, R[("g8", lc // 8)])
                t.wait_at(semV, R[("Wf", min(ch - 1, nch - 1))])
                t.wait_at(semV, R[("evacB", lc - 1)])
                t.wait_at(semS, R[("evacBs", lc - 1)])
                for db in range(8):
                    xb = lfb[:, db * 128:(db + 1) * 128]
                    base = db * 2048
                    t.matmul(pe[:, 0:512], xb, web[:, base:base + 512],
                             start=(db == 0), stop=(db == 7))
                    t.matmul(pe[:, 512:1024], xb, web[:, base + 512:base + 1024],
                             start=(db == 0), stop=(db == 7))
                seal_r(t, semT, "grp0", lc)
                v.wait_at(semT, R[("grp0", lc)])
                v.wait_at(semV, R[("gpred", lc - 1)])
                for k in range(4):
                    v.tensor_scalar(out=tmp16[:, (k + 8) * 128:(k + 9) * 128],
                                    in0=pe[:, k * 128:(k + 1) * 128],
                                    scalar1=lgcol(k), scalar2=None, op0=ALU.mult)
                seal_r(v, semV, "evacA", lc)
                s.wait_at(semT, R[("grp0", lc)])
                s.wait_at(semV, R[("gpred", lc - 1)])
                for k in range(4, 8):
                    s.activation(out=tmp16[:, (k + 8) * 128:(k + 9) * 128],
                                 in_=pe[:, k * 128:(k + 1) * 128], func=AF.Identity,
                                 bias=0.0, scale=lgcol(k))
                seal_r(s, semS, "evacAs", lc)
                t.wait_at(semV, R[("evacA", lc)])
                t.wait_at(semS, R[("evacAs", lc)])
                for db in range(8):
                    xb = lfb[:, db * 128:(db + 1) * 128]
                    base = db * 2048 + 1024
                    t.matmul(pe[:, 0:512], xb, web[:, base:base + 512],
                             start=(db == 0), stop=(db == 7))
                    t.matmul(pe[:, 512:1024], xb, web[:, base + 512:base + 1024],
                             start=(db == 0), stop=(db == 7))
                seal_r(t, semT, "grp1", lc)
                if lc >= 1:
                    t.wait_at(semV, R[("gmTe_l", lc - 1)])
                t.transpose(pbt[0:16, 640:768],
                            gm8_r[(lc // 8) % 3][:, (lc % 8) * 16:(lc % 8 + 1) * 16], idn)
                seal_r(t, semT, "gmT", lc)
                v.wait_at(semT, R[("gmT", lc)])
                v.tensor_copy(out=gmT_r[lc % 12], in_=pbt[0:16, 640:768])
                seal_r(v, semV, "gmTe_l", lc)
                v.wait_at(semT, R[("grp1", lc)])
                for k in range(4):
                    v.tensor_scalar(out=tmp16[:, k * 128:(k + 1) * 128],
                                    in0=pe[:, k * 128:(k + 1) * 128],
                                    scalar1=lgcol(8 + k), scalar2=None, op0=ALU.mult)
                seal_r(v, semV, "evacB", lc)

            if act and ch + 1 < nch:
                g.wait_at(semDr[(ch + 1) % 2], dmaD[ch + 1])
                if ch >= 11:
                    g.wait_at(semT, R[("grp1", ch - 11)])
                g.tensor_copy(out=xfb_r[(ch + 1) % 12], in_=xf_r[(ch + 1) % 2])
                seal_r(g, semG, "xfb", ch + 1)

            if lc >= 0:
                s.wait_at(semT, R[("grp1", lc)])
                for k in range(4, 8):
                    s.activation(out=tmp16[:, k * 128:(k + 1) * 128],
                                 in_=pe[:, k * 128:(k + 1) * 128], func=AF.Identity,
                                 bias=0.0, scale=lgcol(8 + k))
                seal_r(s, semS, "evacBs", lc)
            if lc >= 0 and not act:
                pass

            if act:
                t.wait_at(semV, R[("stt3", ch)])
                t.matmul(papp[:1, OFF_U:OFF_U + 128], mnca_r[sl], Eb[0], start=True, stop=True)
                seal_r(t, semT, "umm", ch)
                v.wait_at(semT, R[("umm", ch)])
                v.tensor_copy(out=u_row, in_=papp[0:1, OFF_U:OFF_U + 128])
                v.tensor_tensor(out=s_tile_r[ch % 12], in0=Ef[0], in1=Vn, op=ALU.add)
                seal_r(v, semV, "urow", ch)
                if lc >= 0:
                    stats_a(ch + 1)
                t.wait_at(semV, R[("urow", ch)])
                if lc >= 0:
                    t.wait_at(semS, R[("evacBs", lc)])
                for db in range(8):
                    t.matmul(pe[:, db * 128:(db + 1) * 128],
                             xrbs_r[sl][:, db * 128:(db + 1) * 128], Eb[0],
                             start=(db % 4 == 0), stop=False)
                ur_bc = dataclasses.replace(u_row, ap=type(u_row.ap)([[128, 1], [0, 4], [1, 128]]))
                t.matmul(pe[:, 0:512], negonesr, ur_bc, start=False, stop=True)
                t.matmul(pe[:, 512:1024], negonesr, ur_bc, start=False, stop=True)
                seal_r(t, semT, "gW", ch)
                v.wait_at(semT, R[("gW", ch)])
                v.scalar_tensor_tensor(out=Wf, in0=pe, scalar=1.0, in1=Wf,
                                       op0=ALU.mult, op1=ALU.add)
                seal_r(v, semV, "Wf", ch)
                if lc >= 0:
                    stats_b(ch + 1)
                    stats_c1(ch + 1)
                s.wait_at(semV, R[("Wf", ch)])
                s.activation(out=Wb[:, 0:512], in_=Wf[:, 0:512], func=AF.Identity,
                             bias=0.0, scale=1.0)
                seal_r(s, semS, "WbS0", ch)
                s.activation(out=Wb[:, 512:1024], in_=Wf[:, 512:1024], func=AF.Identity,
                             bias=0.0, scale=1.0)
                seal_r(s, semS, "WbS1", ch)
                if lc >= 0:
                    stats_s1a(ch + 1)
                    stats_s1b(ch + 1)
                    stats_s2(ch + 1)

            if lc >= 0:
                if act:
                    t.wait_at(semV, R[("stt2", ch)])
                elif lc >= 1:
                    t.wait_at(semV, R[("moe3", lc - 1)])
                t.wait_at(semV, R[("gmTe_l", lc)])
                t.matmul(papp[:, OFF_T2:OFF_T2 + 128], gmT_r[lc % 12], csb, start=True, stop=True)
                seal_r(t, semT, "term2", lc)
                v.wait_at(semS, R[("evacBs", lc)])
                v.wait_at(semV, R[("evacB", lc)])
                v.reduce_sum(out=moe, in_=tmp16[:, :].rearrange("p (e f) -> p f e", e=16),
                             axis=mybir.AxisListType.X)
                seal_r(v, semV, "gpred", lc)
                v.wait_at(semT, R[("term2", lc)])
                if act:
                    v.wait_at(semT, R[("umm", ch)])
                if lc >= 1:
                    v.wait_at(semS, R[("eSg", lc - 1)])
                v.wait_at(semV, R[("gpred", lc)])
                v.scalar_tensor_tensor(out=moe3, in0=papp[:, OFF_T2:OFF_T2 + 128],
                                       scalar=epssig8_r[(lc // 8) % 3][:, lc % 8:lc % 8 + 1],
                                       in1=moe, op0=ALU.add, op1=ALU.add)
                seal_r(v, semV, "moe3", lc)

            if act and lc < 0:
                stats_prep(ch + 1)

            if ch + 1 < nch:
                nsl = (ch + 1) % 2
                nfb = xfb_r[(ch + 1) % 12]
                t.wait_at(semV, R[("stt2", ch)])
                t.wait_at(semS, R[("eL", ch)])
                t.wait_at(semG, R[("xfb", ch + 1)])
                for db in range(8):
                    xb = nfb[:, db * 128:(db + 1) * 128]
                    t.matmul(pgram, xb, xb, start=(db == 0), stop=False)
                for db in range(8):
                    xb = nfb[:, db * 128:(db + 1) * 128]
                    t.matmul(pv, xb, wvb[:, db * 128:(db + 1) * 128], start=(db == 0), stop=False)
                for db in range(8):
                    xb = nfb[:, db * 128:(db + 1) * 128]
                    t.matmul(scr[:, OFF_LOG:OFF_LOG + 16], xb, rwb[:, db * 16:(db + 1) * 16],
                             start=(db == 0), stop=False)
                seal_r(t, semT, "sweepbig", ch + 1)
                t.wait_at(semS, R[("sexp", ch + 1)])
                t.wait_at(semV, R[("gmTe", ch)])
                for k in range(5):
                    t.transpose(pbt[0:1, k * 128:(k + 1) * 128], stats5_r[nsl][:, k:k + 1], idn)
                seal_r(t, semT, "transp", ch + 1)

        fl = nch - 1
        s.wait_at(semV, R[("moe3", fl)])
        s.activation(out=eSg, in_=moe3, func=AF.Exp, bias=0.0, scale=-1.0)
        seal_r(s, semS, "eSg", fl)
        s.wait_at(semG, R[("ghs", fl - 1)])
        s.wait_at(semS, R[("eSg", fl)])
        s.activation(out=lnd, in_=eSg, func=AF.Ln, bias=1.0, scale=1.0)
        s.drain()
        s.activation(out=rdsg, in_=lnd, func=AF.Exp, bias=0.0, scale=-1.0)
        seal_r(s, semS, "dsg", fl)
        v.wait_at(semS, R[("dsg", fl)])
        v.wait_at(semG, R[("ghs", fl - 1)])
        v.tensor_tensor(out=h_t, in0=s_tile_r[fl % 12], in1=moe3, op=ALU.mult)
        seal_r(v, semV, "hrdsg", fl)
        g.wait_at(semV, R[("hrdsg", fl)])
        g.tensor_tensor(out=hs, in0=h_t, in1=rdsg, op=ALU.mult)
        seal_r(g, semG, "ghs", fl)
        v.wait_at(semG, R[("ghs", fl)])
        v.wait_at(semO, 16 * fl)
        v.tensor_tensor(out=out_t, in0=hs, in1=xres_r[fl % 13], op=ALU.add)
        seal_r(v, semV, "gout", fl)
        sy.wait_at(semV, R[("gout", fl)])
        sy.dma_start(out=out_h[fl * 128:(fl + 1) * 128, :], in_=out_t).then_inc(semO, 16)

        sy.wait(semO)

    import os as _os
    _snap = {}
    def _chk(tag):
        import copy as _copy
        if _os.environ.get("KDBG_CNT"):
            if _snap:
                k0 = next(iter(_snap))
                if _snap[k0] != cnt:
                    print("COUNTER MISMATCH", k0, "vs", tag)
                    for kk in cnt:
                        if _snap[k0][kk] != cnt[kk]:
                            print("  ", kk, _snap[k0][kk], cnt[kk])
            _snap[tag] = dict(cnt)

    from contextlib import ExitStack
    with nc.Block() as blk:
        @blk.sync
        def _(eng):
            emit({"sync": eng})
            _chk("sync")

        @blk.vector
        def _(eng):
            emit({"vector": eng})
            _chk("vector")

        @blk.tensor
        def _(eng):
            emit({"tensor": eng})
            _chk("tensor")

        @blk.scalar
        def _(eng):
            emit({"scalar": eng})
            _chk("scalar")

        @blk.gpsimd
        def _(eng):
            emit({"gpsimd": eng})
            _chk("gpsimd")

    return nc


def _host_inputs(L0_attn_x, L0_attn_W0, L0_attn_Wkv, L0_attn_bkv, L0_attn_eta,
                 L0_ffn_router_w, L0_ffn_router_b, L0_ffn_expert_w):
    f = np.float32
    x = np.ascontiguousarray(np.asarray(L0_attn_x, f))
    W0 = np.asarray(L0_attn_W0, f)
    Wkv = np.asarray(L0_attn_Wkv, f)
    bkv = np.asarray(L0_attn_bkv, f)
    eta = np.asarray(L0_attn_eta, f).reshape(1, 1)
    rw = np.asarray(L0_ffn_router_w, f)
    rb = np.asarray(L0_ffn_router_b, f)
    We = np.asarray(L0_ffn_expert_w, f)

    xr_h = np.ascontiguousarray(x.transpose(1, 0, 2).reshape(T, D))
    xf_flat = np.ascontiguousarray(xr_h.T)
    xf_h = np.ascontiguousarray(
        xf_flat.reshape(8, 128, N_CH, 128).transpose(2, 0, 1, 3))

    tstep = np.arange(128) // B
    maskl = (tstep[:, None] > tstep[None, :]).astype(ml_dtypes.bfloat16)
    masku = (tstep[:, None] < tstep[None, :]).astype(ml_dtypes.bfloat16)
    idn = np.eye(128, dtype=f).astype(ml_dtypes.bfloat16)
    onesr = np.ones((1, 128), f).astype(ml_dtypes.bfloat16)
    negonesr = (-np.ones((1, 128), f)).astype(ml_dtypes.bfloat16)
    onesc = np.ones((128, 1), f).astype(ml_dtypes.bfloat16)

    in_maps = []
    for j in range(NCORES):
        fs = slice(j * FSL, (j + 1) * FSL)
        w0t = np.ascontiguousarray(W0[fs, :].T.reshape(8, 128, 128))
        wvt = np.ascontiguousarray(Wkv[D + j * FSL:D + (j + 1) * FSL, :].T.reshape(8, 128, 128))
        bkvr = np.ascontiguousarray(bkv[D + j * FSL:D + (j + 1) * FSL].reshape(1, 128))
        rwt = np.ascontiguousarray(rw.T.reshape(8, 128, 16))
        rbr = np.ascontiguousarray(rb.reshape(1, 16))
        wet = np.ascontiguousarray(
            We[:, :, fs].reshape(16, 8, 128, 128).transpose(1, 2, 0, 3).reshape(8, 128, 2048))
        xres = np.ascontiguousarray(xr_h[:, fs])
        in_maps.append({
            "xr": xr_h, "xf": xf_h, "xres": xres,
            "w0t": w0t, "wvt": wvt, "bkvr": bkvr, "eta": eta,
            "rwt": rwt, "rbr": rbr, "wet": wet,
            "idn": idn, "maskl": maskl, "masku": masku,
            "onesr": onesr, "negonesr": negonesr, "onesc": onesc,
        })
    return in_maps


def kernel(L0_attn_x, L0_attn_W0, L0_attn_Wkv, L0_attn_bkv, L0_attn_eta,
           L0_ffn_router_w, L0_ffn_router_b, L0_ffn_expert_w):
    in_maps = _host_inputs(L0_attn_x, L0_attn_W0, L0_attn_Wkv, L0_attn_bkv,
                           L0_attn_eta, L0_ffn_router_w, L0_ffn_router_b,
                           L0_ffn_expert_w)
    nc = build_program()
    res = run_bass_kernel_spmd(nc, in_maps, core_ids=list(range(NCORES)))
    full = np.concatenate([res.results[j]["out"] for j in range(NCORES)], axis=1)
    return np.ascontiguousarray(full.reshape(S, B, D).transpose(1, 0, 2))


# version 15
# speedup vs baseline: 1.0427x; 1.0427x over previous
import dataclasses
import numpy as np
import ml_dtypes
import concourse.bass as bass
import concourse.mybir as mybir
from concourse.bass_utils import run_bass_kernel_spmd

F32 = mybir.dt.float32
BF16 = mybir.dt.bfloat16
AF = mybir.ActivationFunctionType
ALU = mybir.AluOpType

B, S, D = 8, 4096, 1024
T = B * S
LCH = 16
NR = LCH * B
N_CH = S // LCH
NCORES = 8
FSL = 128
E = 16
EPS = 1e-10
LN_EPS = 1e-5
CSC = 2.0 / (B * D)

OFF_ABC = 0
OFF_LOG = 256
OFF_APP = 0
OFF_T2 = 128
OFF_U = 256
LAG = 10


class Sem:
    def __init__(self, h, key, cnt):
        self.h, self.key, self.cnt = h, key, cnt


class MIns:
    __slots__ = ("ins",)
    def __init__(self, ins):
        self.ins = ins
    def then_inc(self, sem, n):
        sem.cnt[sem.key] += n
        if self.ins is not None:
            self.ins.then_inc(sem.h, n)
        return self


class MEng:
    def __init__(self, eng):
        self.eng = eng
        self.last = None
    def wait(self, sem):
        if self.eng is not None:
            self.eng.wait_ge(sem.h, sem.cnt[sem.key])
    def wait_at(self, sem, val):
        if self.eng is not None:
            self.eng.wait_ge(sem.h, val)
    def gap(self):
        if self.eng is not None:
            self.eng.nop(nofuse=True)
    def seal(self, sem, n=1):
        if self.last is None:
            mi = MIns(self.eng.nop() if self.eng is not None else None)
        else:
            mi = self.last
        mi.then_inc(sem, n)
        self.last = None
    def __getattr__(self, name):
        def call(*a, **k):
            if self.eng is None:
                mi = MIns(None)
            else:
                mi = MIns(getattr(self.eng, name)(*a, **k))
            self.last = mi
            return mi
        return call


def build_program(nch=N_CH):
    nc = bass.Bass("TRN2", target_bir_lowering=False)

    xr_h = nc.dram_tensor("xr", [T, D], F32, kind="ExternalInput").ap()
    xf_h = nc.dram_tensor("xf", [N_CH, 8, 128, 128], F32, kind="ExternalInput").ap()
    xres_h = nc.dram_tensor("xres", [T, FSL], F32, kind="ExternalInput").ap()
    w0t_h = nc.dram_tensor("w0t", [8, 128, 128], F32, kind="ExternalInput").ap()
    wvt_h = nc.dram_tensor("wvt", [8, 128, 128], F32, kind="ExternalInput").ap()
    bkvr_h = nc.dram_tensor("bkvr", [1, 128], F32, kind="ExternalInput").ap()
    eta_h = nc.dram_tensor("eta", [1, 1], F32, kind="ExternalInput").ap()
    rwt_h = nc.dram_tensor("rwt", [8, 128, 16], F32, kind="ExternalInput").ap()
    rbr_h = nc.dram_tensor("rbr", [1, 16], F32, kind="ExternalInput").ap()
    wet_h = nc.dram_tensor("wet", [8, 128, 2048], F32, kind="ExternalInput").ap()
    idn_h = nc.dram_tensor("idn", [128, 128], BF16, kind="ExternalInput").ap()
    maskl_h = nc.dram_tensor("maskl", [128, 128], BF16, kind="ExternalInput").ap()
    masku_h = nc.dram_tensor("masku", [128, 128], BF16, kind="ExternalInput").ap()
    onesr_h = nc.dram_tensor("onesr", [1, 128], BF16, kind="ExternalInput").ap()
    negonesr_h = nc.dram_tensor("negonesr", [1, 128], BF16, kind="ExternalInput").ap()
    onesc_h = nc.dram_tensor("onesc", [128, 1], BF16, kind="ExternalInput").ap()
    out_h = nc.dram_tensor("out", [T, FSL], F32, kind="ExternalOutput").ap()
    csum_h = nc.dram_tensor("csum_scratch", [1, 16, 128], F32, kind="Internal").ap()

    sb = nc.alloc_sbuf_tensor
    xr_r = [sb(f"s_xr{i}", [128, 1024], F32).ap() for i in range(2)]
    xf_r = [sb(f"s_xf{i}", [128, 1024], F32).ap() for i in range(2)]
    xres_r = [sb(f"s_xres{i}", [128, 128], F32).ap() for i in range(13)]
    xfb_r = [sb(f"s_xfb{i}", [128, 1024], BF16).ap() for i in range(12)]
    xrbs_r = [sb(f"s_xrbs{i}", [128, 1024], BF16).ap() for i in range(2)]
    Wf = sb("s_Wf", [128, 1024], F32).ap()
    Wb = sb("s_Wb", [128, 1024], BF16).ap()
    wvf = sb("s_wvf", [128, 1024], F32).ap()
    wvb = sb("s_wvb", [128, 1024], BF16).ap()
    rwf = sb("s_rwf", [128, 128], F32).ap()
    rwb = sb("s_rwb", [128, 128], BF16).ap()
    wstg = sb("s_wstg", [128, 4096], F32).ap()
    web = sb("s_web", [128, 16384], BF16).ap()
    idn = sb("s_idn", [128, 128], BF16).ap()
    maskl = sb("s_maskl", [128, 128], BF16).ap()
    masku = sb("s_masku", [128, 128], BF16).ap()
    onesr = sb("s_onesr", [1, 128], BF16).ap()
    negonesr = sb("s_negonesr", [1, 128], BF16).ap()
    onesc = sb("s_onesc", [128, 1], BF16).ap()
    bkvr_f = sb("s_bkvrf", [1, 128], F32).ap()
    bkvr = sb("s_bkvr", [1, 128], BF16).ap()
    rbr_f = sb("s_rbrf", [1, 16], F32).ap()
    rbr = sb("s_rbr", [1, 16], BF16).ap()
    eta_bc = sb("s_eta", [128, 1], F32).ap()
    etan_bc = sb("s_etan", [128, 1], F32).ap()
    wvcol = sb("s_wvcol", [1, 128], BF16).ap()
    wcol0 = sb("s_wcol0", [1, 128], BF16).ap()
    rwsum = sb("s_rwsum", [1, 16], BF16).ap()
    csum_f = sb("s_csumf", [1, 2048], F32).ap()
    csum16 = sb("s_csum16", [16, 128], F32).ap()
    csb = sb("s_csb", [16, 128], BF16).ap()
    bnrec = sb("s_bnrec", [128, 12], F32).ap()
    mv = sb("s_mv", [128, 2], F32).ap()
    t0c = sb("s_t0c", [128, 1], F32).ap()
    lnc = sb("s_lnc", [128, 1], F32).ap()
    a8_r = [sb(f"s_a8{i}", [128, 8], F32).ap() for i in range(2)]
    ra_col = sb("s_racol", [128, 1], F32).ap()
    nca8_r = [sb(f"s_nca8{i}", [128, 8], F32).ap() for i in range(2)]
    negm8_r = [sb(f"s_negm8{i}", [128, 8], F32).ap() for i in range(2)]
    stats5_r = [sb(f"s_stats5{i}", [128, 5], BF16).ap() for i in range(2)]
    m1024c = sb("s_m1024c", [128, 1], F32).ap()
    mnca_r = [sb(f"s_mncab{i}", [128, 1], BF16).ap() for i in range(2)]
    rows_r = [[sb(f"s_row{k}_{i}", [1, 128], BF16).ap() for k in range(5)] for i in range(2)]
    u_row = sb("s_urow", [1, 128], BF16).ap()
    amaskl_r = [sb(f"s_amaskl{i}", [128, 128], BF16).ap() for i in range(2)]
    amasku_r = [sb(f"s_amasku{i}", [128, 128], BF16).ap() for i in range(2)]
    Np_b = [sb(f"s_Np{i}", [128, 128], BF16).ap() for i in range(2)]
    Nt_b = [sb(f"s_Nt{i}", [128, 128], BF16).ap() for i in range(2)]
    Ef = [sb(f"s_Ef{i}", [128, 128], F32).ap() for i in range(2)]
    Eb = [sb(f"s_Eb{i}", [128, 128], BF16).ap() for i in range(2)]
    Vn = sb("s_Vn", [128, 128], F32).ap()
    s_tile_r = [sb(f"s_stile{i}", [128, 128], F32).ap() for i in range(12)]
    eL8_r = [sb(f"s_eL8{i}", [128, 128], F32).ap() for i in range(2)]
    se8_r = [sb(f"s_se8{i}", [128, 8], F32).ap() for i in range(2)]
    epsZ8 = sb("s_epsZ8", [128, 8], F32).ap()
    m88 = sb("s_m88", [128, 64], F32).ap()
    S68 = sb("s_S68", [128, 8], F32).ap()
    den8 = sb("s_den8", [128, 8], F32).ap()
    rS28 = sb("s_rS28", [128, 8], F32).ap()
    rden8 = sb("s_rden8", [128, 8], F32).ap()
    sig8 = sb("s_sig8", [128, 8], F32).ap()
    epssig8_r = [sb(f"s_epssig8{i}", [128, 8], F32).ap() for i in range(3)]
    maskt8 = sb("s_maskt8", [128, 128], F32).ap()
    graw8 = sb("s_graw8", [128, 128], F32).ap()
    gateA8 = sb("s_gateA8", [128, 128], F32).ap()
    gate8_r = [sb(f"s_gate8{i}", [128, 128], F32).ap() for i in range(3)]
    gm8_r = [sb(f"s_gm8{i}", [128, 128], BF16).ap() for i in range(3)]
    gmT_r = [sb(f"s_gmT{i}", [16, 128], BF16).ap() for i in range(12)]
    tmp16 = sb("s_tmp16", [128, 2048], BF16).ap()
    tsum = sb("s_tsum", [128, 1024], F32).ap()
    moe = sb("s_moe", [128, 128], F32).ap()
    moe3 = sb("s_moe3", [128, 128], F32).ap()
    eSg = sb("s_eSg", [128, 128], F32).ap()
    h_t = sb("s_ht", [128, 128], F32).ap()
    lnd = sb("s_lnd", [128, 1024 // 8], F32).ap()
    rdsg = sb("s_rdsg", [128, 128], F32).ap()
    hs = sb("s_hs", [128, 128], F32).ap()
    out_t = sb("s_outt", [128, 128], F32).ap()

    pgram = nc.alloc_psum_tensor("p_gram", [128, 128], F32).ap()
    pxw = nc.alloc_psum_tensor("p_xw", [128, 128], F32).ap()
    pv = nc.alloc_psum_tensor("p_v", [128, 128], F32).ap()
    pe = nc.alloc_psum_tensor("p_e", [128, 1024], F32).ap()
    scr = nc.alloc_psum_tensor("p_scr", [128, 512], F32).ap()
    pbt = nc.alloc_psum_tensor("p_bt", [128, 1024], BF16).ap()
    papp = nc.alloc_psum_tensor("p_app", [128, 512], F32).ap()

    cnt = {}
    semD = Sem(nc.alloc_semaphore(name="semD"), "D", cnt)
    semD0 = Sem(nc.alloc_semaphore(name="semD0"), "D0", cnt)
    semD1 = Sem(nc.alloc_semaphore(name="semD1"), "D1", cnt)
    semDr = [semD0, semD1]
    semT = Sem(nc.alloc_semaphore(name="semT"), "T", cnt)
    semV = Sem(nc.alloc_semaphore(name="semV"), "V", cnt)
    semS = Sem(nc.alloc_semaphore(name="semS"), "S", cnt)
    semG = Sem(nc.alloc_semaphore(name="semG"), "G", cnt)
    semO = Sem(nc.alloc_semaphore(name="semO"), "O", cnt)

    def emit(active):
        for k in ("D", "D0", "D1", "T", "V", "S", "G", "O"):
            cnt[k] = 0
        sy = MEng(active.get("sync"))
        t = MEng(active.get("tensor"))
        v = MEng(active.get("vector"))
        s = MEng(active.get("scalar"))
        g = MEng(active.get("gpsimd"))

        sy.dma_start(out=idn, in_=idn_h).then_inc(semD, 16)
        sy.dma_start(out=maskl, in_=maskl_h).then_inc(semD, 16)
        sy.dma_start(out=masku, in_=masku_h).then_inc(semD, 16)
        sy.dma_start(out=onesr, in_=onesr_h).then_inc(semD, 16)
        sy.dma_start(out=negonesr, in_=negonesr_h).then_inc(semD, 16)
        sy.dma_start(out=onesc, in_=onesc_h).then_inc(semD, 16)
        sy.dma_start(out=bkvr_f, in_=bkvr_h).then_inc(semD, 16)
        sy.dma_start(out=rbr_f, in_=rbr_h).then_inc(semD, 16)
        sy.dma_start(out=eta_bc, in_=eta_h.to_broadcast((128, 1))).then_inc(semD, 16)
        for db in range(8):
            sy.dma_start(out=Wf[:, db * 128:(db + 1) * 128], in_=w0t_h[db]).then_inc(semD, 16)
            sy.dma_start(out=wvf[:, db * 128:(db + 1) * 128], in_=wvt_h[db]).then_inc(semD, 16)
            sy.dma_start(out=rwf[:, db * 16:(db + 1) * 16], in_=rwt_h[db]).then_inc(semD, 16)
        for q in range(4):
            if q > 0:
                sy.wait(semV)
            sy.dma_start(out=wstg[:, 0:2048], in_=wet_h[2 * q]).then_inc(semD, 16)
            sy.dma_start(out=wstg[:, 2048:4096], in_=wet_h[2 * q + 1]).then_inc(semD, 16)
            v.wait(semD)
            v.tensor_copy(out=web[:, q * 4096:(q + 1) * 4096], in_=wstg[:, :])
            v.seal(semV)
        v.wait(semD)
        v.tensor_copy(out=Wb, in_=Wf)
        v.tensor_copy(out=wvb, in_=wvf)
        v.tensor_copy(out=rwb, in_=rwf)
        v.tensor_copy(out=bkvr, in_=bkvr_f)
        v.tensor_copy(out=rbr, in_=rbr_f)
        v.tensor_scalar(out=etan_bc, in0=eta_bc, scalar1=-CSC, scalar2=None, op0=ALU.mult)
        v.seal(semV)
        t.wait(semV)
        for db in range(8):
            t.matmul(scr[:1, 0:128], onesc, Wb[:, db * 128:(db + 1) * 128],
                     start=(db == 0), stop=(db == 7))
        for db in range(8):
            t.matmul(scr[:1, 128:256], onesc, wvb[:, db * 128:(db + 1) * 128],
                     start=(db == 0), stop=(db == 7))
        for db in range(8):
            t.matmul(scr[:1, 256:272], onesc, rwb[:, db * 16:(db + 1) * 16],
                     start=(db == 0), stop=(db == 7))
        t.seal(semT)
        v.wait(semT)
        v.tensor_copy(out=wcol0, in_=scr[0:1, 0:128])
        v.tensor_copy(out=wvcol, in_=scr[0:1, 128:256])
        v.tensor_copy(out=rwsum, in_=scr[0:1, 256:272])
        v.seal(semV)
        for q in range(4):
            t.wait(semV)
            for db in range(8):
                t.matmul(scr[:1, 0:512], onesc, web[:, db * 2048 + q * 512: db * 2048 + (q + 1) * 512],
                         start=(db == 0), stop=(db == 7))
            t.seal(semT)
            v.wait(semT)
            v.tensor_copy(out=csum_f[:, q * 512:(q + 1) * 512], in_=scr[0:1, 0:512])
            v.seal(semV)
        sy.wait(semV)
        sy.dma_start(out=csum_h, in_=csum_f[:, :].rearrange("o (e f) -> o e f", e=16)).then_inc(semD, 16)
        sy.wait(semD)
        sy.dma_start(out=csum16, in_=csum_h[0]).then_inc(semD, 16)
        v.wait(semD)
        v.tensor_copy(out=csb, in_=csum16)
        v.seal(semV)
        t.wait(semV)

        R = {}

        def seal_r(eng, sem, name, c):
            eng.seal(sem)
            R[(name, c)] = cnt[sem.key]

        def bc816(ap2d):
            return dataclasses.replace(ap2d, ap=type(ap2d.ap)([[8, 128], [1, 8], [0, 16]]))

        R[("Wb", -1)] = cnt["V"]
        R[("WbS", -1)] = cnt["S"]

        def issue_loads(c):
            if c >= nch:
                return
            sl = c % 2
            sd = semDr[sl]
            if c >= 2:
                sy.wait_at(semV, R[("bnsA", c - 2)])
                sy.wait_at(semS, R[("sexp2", c - 2)])
                sy.wait_at(semG, R[("xfb", c - 2)])
            if c >= 13:
                sy.wait_at(semV, R[("gout", c - 13)])
            sy.dma_start(out=xr_r[sl], in_=xr_h[c * 128:(c + 1) * 128, :]).then_inc(sd, 16)
            sy.dma_start(out=xf_r[sl][:, :].rearrange("p (a b) -> p a b", a=8),
                         in_=xf_h[c].transpose([1, 0, 2])).then_inc(sd, 16)
            sy.dma_start(out=xres_r[c % 13], in_=xres_h[c * 128:(c + 1) * 128, :]).then_inc(sd, 16)
            dmaD[c] = cnt[sd.key]

        def stats_prep(c):
            stats_a(c)
            stats_b(c)
            stats_c1(c)
            stats_s1a(c)
            stats_s1b(c)
            stats_s2(c)

        def stats_a(c):
            if c >= nch:
                return
            sl = c % 2
            v.wait_at(semDr[c % 2], dmaD[c])
            v.bn_stats(out=bnrec[:, 0:6], in_=xr_r[sl][:, 0:512])
            v.bn_stats(out=bnrec[:, 6:12], in_=xr_r[sl][:, 512:1024])
            seal_r(v, semV, "bnsA", c)

        def stats_b(c):
            if c >= nch:
                return
            v.wait_at(semV, R[("bnsA", c)])
            if c >= 1:
                v.wait_at(semS, R[("sexp2", c - 1)])
            v.bn_aggr(out=mv, in_=bnrec[:, :].rearrange("p (n r) -> p n r", r=3))
            seal_r(v, semV, "bnagg", c)

        def stats_c1(c):
            if c >= nch:
                return
            b2 = (c // 8) % 2
            c8 = c % 8
            sl = c % 2
            v.wait_at(semV, R[("bnagg", c)])
            if c >= 1:
                v.wait_at(semS, R[("sexp", c - 1)])
            if c >= 2:
                v.wait_at(semT, R[("transp", c - 2)])
            v.tensor_scalar(out=t0c, in0=mv[:, 1:2], scalar1=LN_EPS, scalar2=None, op0=ALU.add)
            v.tensor_scalar(out=negm8_r[b2][:, c8:c8 + 1], in0=mv[:, 0:1], scalar1=-1.0,
                            scalar2=None, op0=ALU.mult)
            v.tensor_scalar(out=m1024c, in0=mv[:, 0:1], scalar1=-1024.0, scalar2=None, op0=ALU.mult)
            v.tensor_copy(out=stats5_r[sl][:, 0:1], in_=mv[:, 0:1])
            v.tensor_scalar(out=stats5_r[sl][:, 3:4], in0=mv[:, 0:1], scalar1=-1024.0,
                            scalar2=None, op0=ALU.mult)
            v.tensor_scalar(out=stats5_r[sl][:, 4:5], in0=mv[:, 0:1], scalar1=-1.0,
                            scalar2=None, op0=ALU.mult)
            seal_r(v, semV, "bnT", c)

        def stats_s1a(c):
            if c >= nch:
                return
            b2 = (c // 8) % 2
            c8 = c % 8
            s.wait_at(semV, R[("bnT", c)])
            if c >= 2:
                s.wait_at(semV, R[("urow", c - 2)])
            if c8 == 0 and c // 8 >= 2:
                s.wait_at(semV, R[("g8", c // 8 - 2)])
            s.activation(out=lnc, in_=t0c, func=AF.Ln, bias=0.0, scale=1.0)
            s.drain()
            s.activation(out=a8_r[b2][:, c8:c8 + 1], in_=lnc, func=AF.Exp, bias=0.0, scale=-0.5)
            s.activation(out=ra_col, in_=lnc, func=AF.Exp, bias=0.0, scale=0.5)
            seal_r(s, semS, "sexpA", c)

        def stats_s1b(c):
            if c >= nch:
                return
            sl = c % 2
            b2 = (c // 8) % 2
            c8 = c % 8
            s.wait_at(semS, R[("sexpA", c)])
            if c >= 2:
                s.wait_at(semT, R[("transp", c - 2)])
            s.activation(out=stats5_r[sl][:, 1:2], in_=a8_r[b2][:, c8:c8 + 1], func=AF.Identity,
                         bias=0.0, scale=1.0)
            s.activation(out=stats5_r[sl][:, 2:3], in_=ra_col, func=AF.Identity,
                         bias=0.0, scale=1.0)
            seal_r(s, semS, "sexp", c)

        def stats_s2(c):
            if c >= nch:
                return
            sl = c % 2
            b2 = (c // 8) % 2
            c8 = c % 8
            s.wait_at(semS, R[("sexp", c)])
            if c >= 2:
                s.wait_at(semT, R[("gW", c - 2)])
            s.activation(out=nca8_r[b2][:, c8:c8 + 1], in_=a8_r[b2][:, c8:c8 + 1],
                         func=AF.Identity, bias=0.0, scale=etan_bc)
            s.drain()
            s.activation(out=mnca_r[sl], in_=nca8_r[b2][:, c8:c8 + 1], func=AF.Identity,
                         bias=0.0, scale=mv[:, 0:1])
            s.activation(out=xrbs_r[sl], in_=xr_r[sl], func=AF.Identity,
                         bias=0.0, scale=nca8_r[b2][:, c8:c8 + 1])
            seal_r(s, semS, "sexp2", c)

        dmaD = {}
        issue_loads(0)
        g.wait_at(semDr[0], dmaD[0])
        g.tensor_copy(out=xfb_r[0], in_=xf_r[0])
        seal_r(g, semG, "xfb", 0)
        stats_prep(0)
        t.wait_at(semG, R[("xfb", 0)])
        for db in range(8):
            xb0 = xfb_r[0][:, db * 128:(db + 1) * 128]
            t.matmul(pgram, xb0, xb0, start=(db == 0), stop=False)
        for db in range(8):
            xb0 = xfb_r[0][:, db * 128:(db + 1) * 128]
            t.matmul(pv, xb0, wvb[:, db * 128:(db + 1) * 128], start=(db == 0), stop=False)
        for db in range(8):
            xb0 = xfb_r[0][:, db * 128:(db + 1) * 128]
            t.matmul(scr[:, OFF_LOG:OFF_LOG + 16], xb0, rwb[:, db * 16:(db + 1) * 16],
                     start=(db == 0), stop=False)
        seal_r(t, semT, "sweepbig", 0)
        t.wait_at(semS, R[("sexp", 0)])
        for k in range(5):
            t.transpose(pbt[0:1, k * 128:(k + 1) * 128], stats5_r[0][:, k:k + 1], idn)
        seal_r(t, semT, "transp", 0)

        for ch in range(nch + LAG):
            act = ch < nch
            lc = ch - LAG
            sl = ch % 2
            b2 = (ch // 8) % 2
            c8 = ch % 8
            xfb = xfb_r[ch % 12]
            acol = a8_r[b2][:, c8:c8 + 1]
            ncacol = nca8_r[b2][:, c8:c8 + 1]
            if lc >= 0:
                lbk = lc // 8
                lgate = gate8_r[lbk % 3]
                lgcol = lambda e: lgate[:, (lc % 8) * 16 + e:(lc % 8) * 16 + e + 1]

            if act:
                issue_loads(ch + 1)

            if c8 == 0 and ch // 8 >= 1 and (ch // 8 - 1) < nch // 8:
                bk = ch // 8 - 1
                eL8 = eL8_r[bk % 2]
                if bk >= 3:
                    v.wait_at(semT, R[("term2", 8 * (bk - 2) - 1)])
                    v.wait_at(semS, R[("evacBs", 8 * (bk - 2) - 1)])
                v.wait_at(semS, R[("eL", 8 * bk + 7)])
                for k in range(8):
                    v.max(out=m88[:, k * 8:(k + 1) * 8], in_=eL8[:, k * 16:(k + 1) * 16])
                v.tensor_scalar(out=epsZ8, in0=se8_r[bk % 2], scalar1=EPS, scalar2=None, op0=ALU.mult)
                v.drain()
                v.reduce_sum(out=S68, in_=m88[:, :].rearrange("p (k e) -> p k e", k=8)[:, :, 0:6],
                             axis=mybir.AxisListType.X)
                m5 = dataclasses.replace(m88[:, :], offset=m88.offset + 5,
                                         ap=type(m88.ap)([[64, 128], [8, 8], [0, 16]]))
                v.tensor_tensor(out=maskt8, in0=eL8[:, :].rearrange("p (k e) -> p k e", k=8),
                                in1=m5, op=ALU.is_ge)
                v.drain()
                v.reciprocal(out=rS28, in_=S68)
                v.tensor_tensor(out=den8, in0=S68, in1=epsZ8, op=ALU.add)
                v.tensor_tensor(out=graw8, in0=maskt8, in1=eL8, op=ALU.mult)
                v.drain()
                v.tensor_tensor(out=gateA8, in0=graw8[:, :].rearrange("p (k e) -> p k e", k=8),
                                in1=bc816(rS28[:, :]), op=ALU.mult)
                v.reciprocal(out=rden8, in_=den8)
                v.drain()
                v.tensor_tensor(out=gate8_r[bk % 3],
                                in0=gateA8[:, :].rearrange("p (k e) -> p k e", k=8),
                                in1=bc816(a8_r[bk % 2][:, :]), op=ALU.mult)
                v.tensor_tensor(out=sig8, in0=S68, in1=rden8, op=ALU.mult)
                v.drain()
                v.tensor_tensor(out=gm8_r[bk % 3],
                                in0=gate8_r[bk % 3][:, :].rearrange("p (k e) -> p k e", k=8),
                                in1=bc816(negm8_r[bk % 2][:, :]), op=ALU.mult)
                v.tensor_scalar(out=epssig8_r[bk % 3], in0=sig8, scalar1=EPS, scalar2=None, op0=ALU.mult)
                seal_r(v, semV, "g8", bk)

            if act:
                s.wait_at(semT, R[("transp", ch)])
                for k in range(5):
                    s.activation(out=rows_r[sl][k], in_=pbt[0:1, k * 128:(k + 1) * 128],
                                 func=AF.Identity, bias=0.0, scale=1.0)
                seal_r(s, semS, "rows", ch)

            if act and lc >= 1:
                s.wait_at(semV, R[("moe3", lc - 1)])
                s.activation(out=eSg, in_=moe3, func=AF.Exp, bias=0.0, scale=-1.0)
                seal_r(s, semS, "eSg", lc - 1)
                if lc >= 2:
                    s.wait_at(semG, R[("ghs", lc - 2)])
                s.wait_at(semS, R[("eSg", lc - 1)])
                s.activation(out=lnd, in_=eSg, func=AF.Ln, bias=1.0, scale=1.0)
                s.drain()
                s.activation(out=rdsg, in_=lnd, func=AF.Exp, bias=0.0, scale=-1.0)
                seal_r(s, semS, "dsg", lc - 1)

            if act:
                rows = rows_r[sl]
                t.wait_at(semS, R[("rows", ch)])
                t.matmul(scr[:, OFF_LOG:OFF_LOG + 16], rows[4], rwsum, start=False, stop=False)
                t.matmul(scr[:, OFF_LOG:OFF_LOG + 16], rows[2], rbr, start=False, stop=True)
                t.matmul(scr[:, OFF_ABC:OFF_ABC + 128], onesr, rows[1], start=True, stop=True)
                seal_r(t, semT, "abc", ch)
                t.matmul(pgram, rows[3], rows[0], start=False, stop=True)
                t.matmul(pv, rows[4], wvcol, start=False, stop=False)
                t.matmul(pv, rows[2], bkvr, start=False, stop=True)
                if ch >= 1:
                    t.wait_at(semV, R[("H", ch - 1)])
                t.wait_at(semS, R[("WbS", ch - 1)])
                for db in range(8):
                    xb = xfb[:, db * 128:(db + 1) * 128]
                    t.matmul(pxw, xb, Wb[:, db * 128:(db + 1) * 128], start=(db == 0), stop=False)
                t.matmul(pxw, rows[4], wcol0, start=False, stop=True)
                seal_r(t, semT, "k1s", ch)
                if lc >= 0:
                    lfb0 = xfb_r[lc % 12]
                    t.wait_at(semV, R[("g8", lc // 8)])
                    for db in range(4):
                        xb = lfb0[:, db * 128:(db + 1) * 128]
                        base = db * 2048
                        t.matmul(pe[:, 0:512], xb, web[:, base:base + 512],
                                 start=(db == 0), stop=False)
                        t.matmul(pe[:, 512:1024], xb, web[:, base + 512:base + 1024],
                                 start=(db == 0), stop=False)

            if act:
                v.wait_at(semT, R[("abc", ch)])
                v.tensor_tensor(out=amaskl_r[sl], in0=scr[:, OFF_ABC:OFF_ABC + 128],
                                in1=maskl, op=ALU.mult)
                v.tensor_tensor(out=amasku_r[sl], in0=scr[:, OFF_ABC:OFF_ABC + 128],
                                in1=masku, op=ALU.mult)
                seal_r(v, semV, "amask", ch)
                v.wait_at(semT, R[("k1s", ch)])
                v.tensor_scalar(out=Vn, in0=pv, scalar1=acol, scalar2=None, op0=ALU.mult)
                seal_r(v, semV, "vn", ch)
                v.wait_at(semV, R[("amask", ch)])
                v.scalar_tensor_tensor(out=Np_b[0], in0=pgram, scalar=ncacol, in1=amaskl_r[sl],
                                       op0=ALU.mult, op1=ALU.mult)
                v.scalar_tensor_tensor(out=Nt_b[0], in0=pgram, scalar=ncacol, in1=amasku_r[sl],
                                       op0=ALU.mult, op1=ALU.mult)
                v.wait_at(semV, R[("vn", ch)])
                v.scalar_tensor_tensor(out=Ef[0], in0=pxw, scalar=acol, in1=Vn,
                                       op0=ALU.mult, op1=ALU.subtract)
                v.scalar_tensor_tensor(out=Eb[0], in0=pxw, scalar=acol, in1=Vn,
                                       op0=ALU.mult, op1=ALU.subtract)
                seal_r(v, semV, "H", ch)

            if act:
                s.wait_at(semT, R[("k1s", ch)])
                s.wait_at(semV, R[("amask", ch)])
                if c8 == 0 and ch // 8 >= 2:
                    s.wait_at(semV, R[("g8", ch // 8 - 2)])
                s.activation(out=eL8_r[b2][:, c8 * 16:(c8 + 1) * 16],
                             in_=scr[:, OFF_LOG:OFF_LOG + 16], func=AF.Exp,
                             bias=0.0, scale=acol, accum_out=se8_r[b2][:, c8:c8 + 1])
                seal_r(s, semS, "eL", ch)

            if act:
                for j in range(4):
                    cur, nxt = j % 2, (j + 1) % 2
                    t.wait_at(semV, R[("H", ch)] if j == 0 else R[(f"stt{j-1}", ch)])
                    t.matmul(papp[:, OFF_APP:OFF_APP + 128], Nt_b[cur], Eb[cur],
                             start=True, stop=True)
                    if j < 3:
                        t.matmul(pgram, Nt_b[cur], Np_b[cur], start=True, stop=True)
                        t.matmul(pv, Np_b[cur], Nt_b[cur], start=True, stop=True)
                    seal_r(t, semT, f"app{j}", ch)
                    if lc >= 0 and j == 0:
                        lfb = xfb_r[lc % 12]
                        for db in range(4, 8):
                            xb = lfb[:, db * 128:(db + 1) * 128]
                            base = db * 2048
                            t.matmul(pe[:, 0:512], xb, web[:, base:base + 512],
                                     start=False, stop=(db == 7))
                            t.matmul(pe[:, 512:1024], xb, web[:, base + 512:base + 1024],
                                     start=False, stop=(db == 7))
                        seal_r(t, semT, "grp0", lc)
                    if lc >= 0 and j == 2:
                        t.wait_at(semS, R[("rows", ch)])
                        t.transpose(pbt[0:16, 640:768],
                                    gm8_r[(lc // 8) % 3][:, (lc % 8) * 16:(lc % 8 + 1) * 16], idn)
                        seal_r(t, semT, "gmT", lc)
                    if lc >= 0 and j in (2, 3):
                        lfb = xfb_r[lc % 12]
                        if j == 2:
                            t.wait_at(semV, R[("evacA", lc)])
                            t.wait_at(semS, R[("evacAs", lc)])
                        for db in range(4 * (j - 2), 4 * (j - 2) + 4):
                            xb = lfb[:, db * 128:(db + 1) * 128]
                            base = db * 2048 + 1024
                            t.matmul(pe[:, 0:512], xb, web[:, base:base + 512],
                                     start=(db == 0), stop=(db == 7))
                            t.matmul(pe[:, 512:1024], xb, web[:, base + 512:base + 1024],
                                     start=(db == 0), stop=(db == 7))
                        if j == 3:
                            seal_r(t, semT, "grp1", lc)
                    v.wait_at(semT, R[(f"app{j}", ch)])
                    v.scalar_tensor_tensor(out=Ef[nxt], in0=papp[:, OFF_APP:OFF_APP + 128],
                                           scalar=1.0, in1=Ef[cur], op0=ALU.mult, op1=ALU.add)
                    v.scalar_tensor_tensor(out=Eb[nxt], in0=papp[:, OFF_APP:OFF_APP + 128],
                                           scalar=1.0, in1=Ef[cur], op0=ALU.mult, op1=ALU.add)
                    if j < 3:
                        v.tensor_copy(out=Np_b[nxt], in_=pgram)
                        v.tensor_copy(out=Nt_b[nxt], in_=pv)
                    if lc >= 0 and j == 2:
                        v.wait_at(semT, R[("gmT", lc)])
                        v.tensor_copy(out=gmT_r[lc % 12], in_=pbt[0:16, 640:768])
                    if lc >= 0 and j == 3:
                        v.wait_at(semT, R[("grp1", lc)])
                        for k in range(4):
                            v.tensor_scalar(out=tmp16[:, k * 128:(k + 1) * 128],
                                            in0=pe[:, k * 128:(k + 1) * 128],
                                            scalar1=lgcol(8 + k), scalar2=None, op0=ALU.mult)
                    seal_r(v, semV, f"stt{j}", ch)
                    if lc >= 1 and j == 0:
                        v.wait_at(semS, R[("dsg", lc - 1)])
                        if lc >= 2:
                            v.wait_at(semG, R[("ghs", lc - 2)])
                        v.tensor_tensor(out=h_t, in0=s_tile_r[(lc - 1) % 12], in1=moe3,
                                        op=ALU.mult)
                        seal_r(v, semV, "hrdsg", lc - 1)
                    if lc >= 0 and j == 1:
                        v.wait_at(semT, R[("grp0", lc)])
                        if lc >= 1:
                            v.wait_at(semV, R[("gpred", lc - 1)])
                        for k in range(4):
                            v.tensor_scalar(out=tmp16[:, (k + 8) * 128:(k + 9) * 128],
                                            in0=pe[:, k * 128:(k + 1) * 128],
                                            scalar1=lgcol(k), scalar2=None, op0=ALU.mult)
                        seal_r(v, semV, "evacA", lc)
                        s.wait_at(semT, R[("grp0", lc)])
                        if lc >= 1:
                            s.wait_at(semV, R[("gpred", lc - 1)])
                        for k in range(4, 8):
                            s.activation(out=tmp16[:, (k + 8) * 128:(k + 9) * 128],
                                         in_=pe[:, k * 128:(k + 1) * 128], func=AF.Identity,
                                         bias=0.0, scale=lgcol(k))
                        seal_r(s, semS, "evacAs", lc)
                    if lc >= 1 and j == 1:
                        g.wait_at(semV, R[("hrdsg", lc - 1)])
                        g.tensor_tensor(out=hs, in0=h_t, in1=rdsg, op=ALU.mult)
                        seal_r(g, semG, "ghs", lc - 1)
                        v.wait_at(semG, R[("ghs", lc - 1)])
                        if lc >= 2:
                            v.wait_at(semO, 16 * (lc - 1))
                        v.tensor_tensor(out=out_t, in0=hs, in1=xres_r[(lc - 1) % 13],
                                        op=ALU.add)
                        seal_r(v, semV, "gout", lc - 1)
                        sy.wait_at(semV, R[("gout", lc - 1)])
                        sy.dma_start(out=out_h[(lc - 1) * 128:lc * 128, :],
                                     in_=out_t).then_inc(semO, 16)
                    if lc >= 0 and j == 2:
                        R[("gmTe_l", lc)] = R[("stt2", ch)]
                    if lc >= 0 and j == 3:
                        R[("evacB", lc)] = R[("stt3", ch)]
                R[("gmTe", ch)] = R[("stt2", ch)]
            else:
                s.wait_at(semV, R[("moe3", lc - 1)])
                s.activation(out=eSg, in_=moe3, func=AF.Exp, bias=0.0, scale=-1.0)
                seal_r(s, semS, "eSg", lc - 1)
                s.wait_at(semG, R[("ghs", lc - 2)])
                s.wait_at(semS, R[("eSg", lc - 1)])
                s.activation(out=lnd, in_=eSg, func=AF.Ln, bias=1.0, scale=1.0)
                s.drain()
                s.activation(out=rdsg, in_=lnd, func=AF.Exp, bias=0.0, scale=-1.0)
                seal_r(s, semS, "dsg", lc - 1)
                v.wait_at(semS, R[("dsg", lc - 1)])
                v.wait_at(semG, R[("ghs", lc - 2)])
                v.tensor_tensor(out=h_t, in0=s_tile_r[(lc - 1) % 12], in1=moe3, op=ALU.mult)
                seal_r(v, semV, "hrdsg", lc - 1)
                g.wait_at(semV, R[("hrdsg", lc - 1)])
                g.tensor_tensor(out=hs, in0=h_t, in1=rdsg, op=ALU.mult)
                seal_r(g, semG, "ghs", lc - 1)
                v.wait_at(semG, R[("ghs", lc - 1)])
                v.wait_at(semO, 16 * (lc - 1))
                v.tensor_tensor(out=out_t, in0=hs, in1=xres_r[(lc - 1) % 13], op=ALU.add)
                seal_r(v, semV, "gout", lc - 1)
                sy.wait_at(semV, R[("gout", lc - 1)])
                sy.dma_start(out=out_h[(lc - 1) * 128:lc * 128, :],
                             in_=out_t).then_inc(semO, 16)
                lfb = xfb_r[lc % 12]
                t.wait_at(semV, R[("g8", lc // 8)])
                t.wait_at(semV, R[("Wf", min(ch - 1, nch - 1))])
                t.wait_at(semV, R[("evacB", lc - 1)])
                t.wait_at(semS, R[("evacBs", lc - 1)])
                for db in range(8):
                    xb = lfb[:, db * 128:(db + 1) * 128]
                    base = db * 2048
                    t.matmul(pe[:, 0:512], xb, web[:, base:base + 512],
                             start=(db == 0), stop=(db == 7))
                    t.matmul(pe[:, 512:1024], xb, web[:, base + 512:base + 1024],
                             start=(db == 0), stop=(db == 7))
                seal_r(t, semT, "grp0", lc)
                v.wait_at(semT, R[("grp0", lc)])
                v.wait_at(semV, R[("gpred", lc - 1)])
                for k in range(4):
                    v.tensor_scalar(out=tmp16[:, (k + 8) * 128:(k + 9) * 128],
                                    in0=pe[:, k * 128:(k + 1) * 128],
                                    scalar1=lgcol(k), scalar2=None, op0=ALU.mult)
                seal_r(v, semV, "evacA", lc)
                s.wait_at(semT, R[("grp0", lc)])
                s.wait_at(semV, R[("gpred", lc - 1)])
                for k in range(4, 8):
                    s.activation(out=tmp16[:, (k + 8) * 128:(k + 9) * 128],
                                 in_=pe[:, k * 128:(k + 1) * 128], func=AF.Identity,
                                 bias=0.0, scale=lgcol(k))
                seal_r(s, semS, "evacAs", lc)
                t.wait_at(semV, R[("evacA", lc)])
                t.wait_at(semS, R[("evacAs", lc)])
                for db in range(8):
                    xb = lfb[:, db * 128:(db + 1) * 128]
                    base = db * 2048 + 1024
                    t.matmul(pe[:, 0:512], xb, web[:, base:base + 512],
                             start=(db == 0), stop=(db == 7))
                    t.matmul(pe[:, 512:1024], xb, web[:, base + 512:base + 1024],
                             start=(db == 0), stop=(db == 7))
                seal_r(t, semT, "grp1", lc)
                if lc >= 1:
                    t.wait_at(semV, R[("gmTe_l", lc - 1)])
                t.transpose(pbt[0:16, 640:768],
                            gm8_r[(lc // 8) % 3][:, (lc % 8) * 16:(lc % 8 + 1) * 16], idn)
                seal_r(t, semT, "gmT", lc)
                v.wait_at(semT, R[("gmT", lc)])
                v.tensor_copy(out=gmT_r[lc % 12], in_=pbt[0:16, 640:768])
                seal_r(v, semV, "gmTe_l", lc)
                v.wait_at(semT, R[("grp1", lc)])
                for k in range(4):
                    v.tensor_scalar(out=tmp16[:, k * 128:(k + 1) * 128],
                                    in0=pe[:, k * 128:(k + 1) * 128],
                                    scalar1=lgcol(8 + k), scalar2=None, op0=ALU.mult)
                seal_r(v, semV, "evacB", lc)

            if act and ch + 1 < nch:
                g.wait_at(semDr[(ch + 1) % 2], dmaD[ch + 1])
                if ch >= 11:
                    g.wait_at(semT, R[("grp1", ch - 11)])
                g.tensor_copy(out=xfb_r[(ch + 1) % 12], in_=xf_r[(ch + 1) % 2])
                seal_r(g, semG, "xfb", ch + 1)

            if lc >= 0:
                s.wait_at(semT, R[("grp1", lc)])
                for k in range(4, 8):
                    s.activation(out=tmp16[:, k * 128:(k + 1) * 128],
                                 in_=pe[:, k * 128:(k + 1) * 128], func=AF.Identity,
                                 bias=0.0, scale=lgcol(8 + k))
                seal_r(s, semS, "evacBs", lc)
            if lc >= 0 and not act:
                pass

            if act:
                t.wait_at(semV, R[("stt3", ch)])
                t.matmul(papp[:1, OFF_U:OFF_U + 128], mnca_r[sl], Eb[0], start=True, stop=True)
                seal_r(t, semT, "umm", ch)
                v.wait_at(semT, R[("umm", ch)])
                v.tensor_copy(out=u_row, in_=papp[0:1, OFF_U:OFF_U + 128])
                v.tensor_tensor(out=s_tile_r[ch % 12], in0=Ef[0], in1=Vn, op=ALU.add)
                seal_r(v, semV, "urow", ch)
                if lc >= 0:
                    stats_a(ch + 1)
                t.wait_at(semV, R[("urow", ch)])
                if lc >= 0:
                    t.wait_at(semS, R[("evacBs", lc)])
                for db in range(8):
                    t.matmul(pe[:, db * 128:(db + 1) * 128],
                             xrbs_r[sl][:, db * 128:(db + 1) * 128], Eb[0],
                             start=(db % 4 == 0), stop=False)
                ur_bc = dataclasses.replace(u_row, ap=type(u_row.ap)([[128, 1], [0, 4], [1, 128]]))
                t.matmul(pe[:, 0:512], negonesr, ur_bc, start=False, stop=True)
                t.matmul(pe[:, 512:1024], negonesr, ur_bc, start=False, stop=True)
                seal_r(t, semT, "gW", ch)
                v.wait_at(semT, R[("gW", ch)])
                v.scalar_tensor_tensor(out=Wf, in0=pe, scalar=1.0, in1=Wf,
                                       op0=ALU.mult, op1=ALU.add)
                seal_r(v, semV, "Wf", ch)
                if lc >= 0:
                    stats_b(ch + 1)
                    stats_c1(ch + 1)
                s.wait_at(semV, R[("Wf", ch)])
                s.activation(out=Wb, in_=Wf, func=AF.Identity, bias=0.0, scale=1.0)
                seal_r(s, semS, "WbS", ch)
                if lc >= 0:
                    stats_s1a(ch + 1)
                    stats_s1b(ch + 1)
                    stats_s2(ch + 1)

            if lc >= 0:
                if act:
                    t.wait_at(semV, R[("stt2", ch)])
                elif lc >= 1:
                    t.wait_at(semV, R[("moe3", lc - 1)])
                t.wait_at(semV, R[("gmTe_l", lc)])
                t.matmul(papp[:, OFF_T2:OFF_T2 + 128], gmT_r[lc % 12], csb, start=True, stop=True)
                seal_r(t, semT, "term2", lc)
                v.wait_at(semS, R[("evacBs", lc)])
                v.wait_at(semV, R[("evacB", lc)])
                v.reduce_sum(out=moe, in_=tmp16[:, :].rearrange("p (e f) -> p f e", e=16),
                             axis=mybir.AxisListType.X)
                seal_r(v, semV, "gpred", lc)
                v.wait_at(semT, R[("term2", lc)])
                if act:
                    v.wait_at(semT, R[("umm", ch)])
                if lc >= 1:
                    v.wait_at(semS, R[("eSg", lc - 1)])
                v.wait_at(semV, R[("gpred", lc)])
                v.scalar_tensor_tensor(out=moe3, in0=papp[:, OFF_T2:OFF_T2 + 128],
                                       scalar=epssig8_r[(lc // 8) % 3][:, lc % 8:lc % 8 + 1],
                                       in1=moe, op0=ALU.add, op1=ALU.add)
                seal_r(v, semV, "moe3", lc)

            if act and lc < 0:
                stats_prep(ch + 1)

            if ch + 1 < nch:
                nsl = (ch + 1) % 2
                nfb = xfb_r[(ch + 1) % 12]
                t.wait_at(semV, R[("stt2", ch)])
                t.wait_at(semS, R[("eL", ch)])
                t.wait_at(semG, R[("xfb", ch + 1)])
                for db in range(8):
                    xb = nfb[:, db * 128:(db + 1) * 128]
                    t.matmul(pgram, xb, xb, start=(db == 0), stop=False)
                for db in range(8):
                    xb = nfb[:, db * 128:(db + 1) * 128]
                    t.matmul(pv, xb, wvb[:, db * 128:(db + 1) * 128], start=(db == 0), stop=False)
                for db in range(8):
                    xb = nfb[:, db * 128:(db + 1) * 128]
                    t.matmul(scr[:, OFF_LOG:OFF_LOG + 16], xb, rwb[:, db * 16:(db + 1) * 16],
                             start=(db == 0), stop=False)
                seal_r(t, semT, "sweepbig", ch + 1)
                t.wait_at(semS, R[("sexp", ch + 1)])
                t.wait_at(semV, R[("gmTe", ch)])
                for k in range(5):
                    t.transpose(pbt[0:1, k * 128:(k + 1) * 128], stats5_r[nsl][:, k:k + 1], idn)
                seal_r(t, semT, "transp", ch + 1)

        fl = nch - 1
        s.wait_at(semV, R[("moe3", fl)])
        s.activation(out=eSg, in_=moe3, func=AF.Exp, bias=0.0, scale=-1.0)
        seal_r(s, semS, "eSg", fl)
        s.wait_at(semG, R[("ghs", fl - 1)])
        s.wait_at(semS, R[("eSg", fl)])
        s.activation(out=lnd, in_=eSg, func=AF.Ln, bias=1.0, scale=1.0)
        s.drain()
        s.activation(out=rdsg, in_=lnd, func=AF.Exp, bias=0.0, scale=-1.0)
        seal_r(s, semS, "dsg", fl)
        v.wait_at(semS, R[("dsg", fl)])
        v.wait_at(semG, R[("ghs", fl - 1)])
        v.tensor_tensor(out=h_t, in0=s_tile_r[fl % 12], in1=moe3, op=ALU.mult)
        seal_r(v, semV, "hrdsg", fl)
        g.wait_at(semV, R[("hrdsg", fl)])
        g.tensor_tensor(out=hs, in0=h_t, in1=rdsg, op=ALU.mult)
        seal_r(g, semG, "ghs", fl)
        v.wait_at(semG, R[("ghs", fl)])
        v.wait_at(semO, 16 * fl)
        v.tensor_tensor(out=out_t, in0=hs, in1=xres_r[fl % 13], op=ALU.add)
        seal_r(v, semV, "gout", fl)
        sy.wait_at(semV, R[("gout", fl)])
        sy.dma_start(out=out_h[fl * 128:(fl + 1) * 128, :], in_=out_t).then_inc(semO, 16)

        sy.wait(semO)

    import os as _os
    _snap = {}
    def _chk(tag):
        import copy as _copy
        if _os.environ.get("KDBG_CNT"):
            if _snap:
                k0 = next(iter(_snap))
                if _snap[k0] != cnt:
                    print("COUNTER MISMATCH", k0, "vs", tag)
                    for kk in cnt:
                        if _snap[k0][kk] != cnt[kk]:
                            print("  ", kk, _snap[k0][kk], cnt[kk])
            _snap[tag] = dict(cnt)

    from contextlib import ExitStack
    with nc.Block() as blk:
        @blk.sync
        def _(eng):
            emit({"sync": eng})
            _chk("sync")

        @blk.vector
        def _(eng):
            emit({"vector": eng})
            _chk("vector")

        @blk.tensor
        def _(eng):
            emit({"tensor": eng})
            _chk("tensor")

        @blk.scalar
        def _(eng):
            emit({"scalar": eng})
            _chk("scalar")

        @blk.gpsimd
        def _(eng):
            emit({"gpsimd": eng})
            _chk("gpsimd")

    return nc


def _host_inputs(L0_attn_x, L0_attn_W0, L0_attn_Wkv, L0_attn_bkv, L0_attn_eta,
                 L0_ffn_router_w, L0_ffn_router_b, L0_ffn_expert_w):
    f = np.float32
    x = np.ascontiguousarray(np.asarray(L0_attn_x, f))
    W0 = np.asarray(L0_attn_W0, f)
    Wkv = np.asarray(L0_attn_Wkv, f)
    bkv = np.asarray(L0_attn_bkv, f)
    eta = np.asarray(L0_attn_eta, f).reshape(1, 1)
    rw = np.asarray(L0_ffn_router_w, f)
    rb = np.asarray(L0_ffn_router_b, f)
    We = np.asarray(L0_ffn_expert_w, f)

    xr_h = np.ascontiguousarray(x.transpose(1, 0, 2).reshape(T, D))
    xf_flat = np.ascontiguousarray(xr_h.T)
    xf_h = np.ascontiguousarray(
        xf_flat.reshape(8, 128, N_CH, 128).transpose(2, 0, 1, 3))

    tstep = np.arange(128) // B
    maskl = (tstep[:, None] > tstep[None, :]).astype(ml_dtypes.bfloat16)
    masku = (tstep[:, None] < tstep[None, :]).astype(ml_dtypes.bfloat16)
    idn = np.eye(128, dtype=f).astype(ml_dtypes.bfloat16)
    onesr = np.ones((1, 128), f).astype(ml_dtypes.bfloat16)
    negonesr = (-np.ones((1, 128), f)).astype(ml_dtypes.bfloat16)
    onesc = np.ones((128, 1), f).astype(ml_dtypes.bfloat16)

    in_maps = []
    for j in range(NCORES):
        fs = slice(j * FSL, (j + 1) * FSL)
        w0t = np.ascontiguousarray(W0[fs, :].T.reshape(8, 128, 128))
        wvt = np.ascontiguousarray(Wkv[D + j * FSL:D + (j + 1) * FSL, :].T.reshape(8, 128, 128))
        bkvr = np.ascontiguousarray(bkv[D + j * FSL:D + (j + 1) * FSL].reshape(1, 128))
        rwt = np.ascontiguousarray(rw.T.reshape(8, 128, 16))
        rbr = np.ascontiguousarray(rb.reshape(1, 16))
        wet = np.ascontiguousarray(
            We[:, :, fs].reshape(16, 8, 128, 128).transpose(1, 2, 0, 3).reshape(8, 128, 2048))
        xres = np.ascontiguousarray(xr_h[:, fs])
        in_maps.append({
            "xr": xr_h, "xf": xf_h, "xres": xres,
            "w0t": w0t, "wvt": wvt, "bkvr": bkvr, "eta": eta,
            "rwt": rwt, "rbr": rbr, "wet": wet,
            "idn": idn, "maskl": maskl, "masku": masku,
            "onesr": onesr, "negonesr": negonesr, "onesc": onesc,
        })
    return in_maps


def kernel(L0_attn_x, L0_attn_W0, L0_attn_Wkv, L0_attn_bkv, L0_attn_eta,
           L0_ffn_router_w, L0_ffn_router_b, L0_ffn_expert_w):
    in_maps = _host_inputs(L0_attn_x, L0_attn_W0, L0_attn_Wkv, L0_attn_bkv,
                           L0_attn_eta, L0_ffn_router_w, L0_ffn_router_b,
                           L0_ffn_expert_w)
    nc = build_program()
    res = run_bass_kernel_spmd(nc, in_maps, core_ids=list(range(NCORES)))
    full = np.concatenate([res.results[j]["out"] for j in range(NCORES)], axis=1)
    return np.ascontiguousarray(full.reshape(S, B, D).transpose(1, 0, 2))


# version 16 (final)
# speedup vs baseline: 1.0867x; 1.0422x over previous
import dataclasses
import numpy as np
import ml_dtypes
import concourse.bass as bass
import concourse.mybir as mybir
from concourse.bass_utils import run_bass_kernel_spmd

F32 = mybir.dt.float32
BF16 = mybir.dt.bfloat16
AF = mybir.ActivationFunctionType
ALU = mybir.AluOpType

B, S, D = 8, 4096, 1024
T = B * S
LCH = 16
NR = LCH * B
N_CH = S // LCH
NCORES = 8
FSL = 128
E = 16
EPS = 1e-10
LN_EPS = 1e-5
CSC = 2.0 / (B * D)

OFF_ABC = 0
OFF_LOG = 256
OFF_APP = 0
OFF_T2 = 128
OFF_U = 256
LAG = 10


class Sem:
    def __init__(self, h, key, cnt):
        self.h, self.key, self.cnt = h, key, cnt


class MIns:
    __slots__ = ("ins",)
    def __init__(self, ins):
        self.ins = ins
    def then_inc(self, sem, n):
        sem.cnt[sem.key] += n
        if self.ins is not None:
            self.ins.then_inc(sem.h, n)
        return self


class MEng:
    def __init__(self, eng):
        self.eng = eng
        self.last = None
    def wait(self, sem):
        if self.eng is not None:
            self.eng.wait_ge(sem.h, sem.cnt[sem.key])
    def wait_at(self, sem, val):
        if self.eng is not None:
            self.eng.wait_ge(sem.h, val)
    def gap(self):
        if self.eng is not None:
            self.eng.nop(nofuse=True)
    def seal(self, sem, n=1):
        if self.last is None:
            mi = MIns(self.eng.nop() if self.eng is not None else None)
        else:
            mi = self.last
        mi.then_inc(sem, n)
        self.last = None
    def __getattr__(self, name):
        def call(*a, **k):
            if self.eng is None:
                mi = MIns(None)
            else:
                mi = MIns(getattr(self.eng, name)(*a, **k))
            self.last = mi
            return mi
        return call


def build_program(nch=N_CH):
    nc = bass.Bass("TRN2", target_bir_lowering=False)

    xr_h = nc.dram_tensor("xr", [T, D], F32, kind="ExternalInput").ap()
    xf_h = nc.dram_tensor("xf", [N_CH, 8, 128, 128], F32, kind="ExternalInput").ap()
    xres_h = nc.dram_tensor("xres", [T, FSL], F32, kind="ExternalInput").ap()
    w0t_h = nc.dram_tensor("w0t", [8, 128, 128], F32, kind="ExternalInput").ap()
    wvt_h = nc.dram_tensor("wvt", [8, 128, 128], F32, kind="ExternalInput").ap()
    bkvr_h = nc.dram_tensor("bkvr", [1, 128], F32, kind="ExternalInput").ap()
    eta_h = nc.dram_tensor("eta", [1, 1], F32, kind="ExternalInput").ap()
    rwt_h = nc.dram_tensor("rwt", [8, 128, 16], F32, kind="ExternalInput").ap()
    rbr_h = nc.dram_tensor("rbr", [1, 16], F32, kind="ExternalInput").ap()
    wet_h = nc.dram_tensor("wet", [8, 128, 2048], F32, kind="ExternalInput").ap()
    idn_h = nc.dram_tensor("idn", [128, 128], BF16, kind="ExternalInput").ap()
    maskl_h = nc.dram_tensor("maskl", [128, 128], BF16, kind="ExternalInput").ap()
    masku_h = nc.dram_tensor("masku", [128, 128], BF16, kind="ExternalInput").ap()
    onesr_h = nc.dram_tensor("onesr", [1, 128], BF16, kind="ExternalInput").ap()
    negonesr_h = nc.dram_tensor("negonesr", [1, 128], BF16, kind="ExternalInput").ap()
    onesc_h = nc.dram_tensor("onesc", [128, 1], BF16, kind="ExternalInput").ap()
    out_h = nc.dram_tensor("out", [T, FSL], F32, kind="ExternalOutput").ap()
    csum_h = nc.dram_tensor("csum_scratch", [1, 16, 128], F32, kind="Internal").ap()

    sb = nc.alloc_sbuf_tensor
    xr_r = [sb(f"s_xr{i}", [128, 1024], F32).ap() for i in range(2)]
    xf_r = [sb(f"s_xf{i}", [128, 1024], F32).ap() for i in range(2)]
    xres_r = [sb(f"s_xres{i}", [128, 128], F32).ap() for i in range(13)]
    xfb_r = [sb(f"s_xfb{i}", [128, 1024], BF16).ap() for i in range(12)]
    xrbs_r = [sb(f"s_xrbs{i}", [128, 1024], BF16).ap() for i in range(2)]
    Wf = sb("s_Wf", [128, 1024], F32).ap()
    Wb = sb("s_Wb", [128, 1024], BF16).ap()
    wvf = sb("s_wvf", [128, 1024], F32).ap()
    wvb = sb("s_wvb", [128, 1024], BF16).ap()
    rwf = sb("s_rwf", [128, 128], F32).ap()
    rwb = sb("s_rwb", [128, 128], BF16).ap()
    wstg = sb("s_wstg", [128, 4096], F32).ap()
    web = sb("s_web", [128, 16384], BF16).ap()
    idn = sb("s_idn", [128, 128], BF16).ap()
    maskl = sb("s_maskl", [128, 128], BF16).ap()
    masku = sb("s_masku", [128, 128], BF16).ap()
    onesr = sb("s_onesr", [1, 128], BF16).ap()
    negonesr = sb("s_negonesr", [1, 128], BF16).ap()
    onesc = sb("s_onesc", [128, 1], BF16).ap()
    bkvr_f = sb("s_bkvrf", [1, 128], F32).ap()
    bkvr = sb("s_bkvr", [1, 128], BF16).ap()
    rbr_f = sb("s_rbrf", [1, 16], F32).ap()
    rbr = sb("s_rbr", [1, 16], BF16).ap()
    eta_bc = sb("s_eta", [128, 1], F32).ap()
    etan_bc = sb("s_etan", [128, 1], F32).ap()
    wvcol = sb("s_wvcol", [1, 128], BF16).ap()
    wcol0 = sb("s_wcol0", [1, 128], BF16).ap()
    rwsum = sb("s_rwsum", [1, 16], BF16).ap()
    csum_f = sb("s_csumf", [1, 2048], F32).ap()
    csum16 = sb("s_csum16", [16, 128], F32).ap()
    csb = sb("s_csb", [16, 128], BF16).ap()
    bnrec = sb("s_bnrec", [128, 12], F32).ap()
    mv = sb("s_mv", [128, 2], F32).ap()
    t0c = sb("s_t0c", [128, 1], F32).ap()
    lnc = sb("s_lnc", [128, 1], F32).ap()
    a8_r = [sb(f"s_a8{i}", [128, 8], F32).ap() for i in range(2)]
    ra_col = sb("s_racol", [128, 1], F32).ap()
    nca8_r = [sb(f"s_nca8{i}", [128, 8], F32).ap() for i in range(2)]
    negm8_r = [sb(f"s_negm8{i}", [128, 8], F32).ap() for i in range(2)]
    stats5_r = [sb(f"s_stats5{i}", [128, 5], BF16).ap() for i in range(2)]
    m1024c = sb("s_m1024c", [128, 1], F32).ap()
    mnca_r = [sb(f"s_mncab{i}", [128, 1], BF16).ap() for i in range(2)]
    rows_r = [[sb(f"s_row{k}_{i}", [1, 128], BF16).ap() for k in range(5)] for i in range(2)]
    u_row = sb("s_urow", [1, 128], BF16).ap()
    amaskl_r = [sb(f"s_amaskl{i}", [128, 128], BF16).ap() for i in range(2)]
    amasku_r = [sb(f"s_amasku{i}", [128, 128], BF16).ap() for i in range(2)]
    Np_b = [sb(f"s_Np{i}", [128, 128], BF16).ap() for i in range(2)]
    Nt_b = [sb(f"s_Nt{i}", [128, 128], BF16).ap() for i in range(2)]
    Ef = [sb(f"s_Ef{i}", [128, 128], F32).ap() for i in range(2)]
    Eb = [sb(f"s_Eb{i}", [128, 128], BF16).ap() for i in range(2)]
    Vn = sb("s_Vn", [128, 128], F32).ap()
    s_tile_r = [sb(f"s_stile{i}", [128, 128], F32).ap() for i in range(12)]
    eL8_r = [sb(f"s_eL8{i}", [128, 128], F32).ap() for i in range(2)]
    se8_r = [sb(f"s_se8{i}", [128, 8], F32).ap() for i in range(2)]
    epsZ8 = sb("s_epsZ8", [128, 8], F32).ap()
    m88 = sb("s_m88", [128, 64], F32).ap()
    S68 = sb("s_S68", [128, 8], F32).ap()
    den8 = sb("s_den8", [128, 8], F32).ap()
    rS28 = sb("s_rS28", [128, 8], F32).ap()
    rden8 = sb("s_rden8", [128, 8], F32).ap()
    sig8 = sb("s_sig8", [128, 8], F32).ap()
    epssig8_r = [sb(f"s_epssig8{i}", [128, 8], F32).ap() for i in range(3)]
    maskt8 = sb("s_maskt8", [128, 128], F32).ap()
    graw8 = sb("s_graw8", [128, 128], F32).ap()
    gateA8 = sb("s_gateA8", [128, 128], F32).ap()
    gate8_r = [sb(f"s_gate8{i}", [128, 128], F32).ap() for i in range(3)]
    gm8_r = [sb(f"s_gm8{i}", [128, 128], BF16).ap() for i in range(3)]
    gmT_r = [sb(f"s_gmT{i}", [16, 128], BF16).ap() for i in range(12)]
    tmp16 = sb("s_tmp16", [128, 2048], BF16).ap()
    tsum = sb("s_tsum", [128, 1024], F32).ap()
    moe = sb("s_moe", [128, 128], F32).ap()
    moe3 = sb("s_moe3", [128, 128], F32).ap()
    eSg = sb("s_eSg", [128, 128], F32).ap()
    h_t = sb("s_ht", [128, 128], F32).ap()
    lnd = sb("s_lnd", [128, 1024 // 8], F32).ap()
    rdsg = sb("s_rdsg", [128, 128], F32).ap()
    hs = sb("s_hs", [128, 128], F32).ap()
    out_t = sb("s_outt", [128, 128], F32).ap()

    pgram = nc.alloc_psum_tensor("p_gram", [128, 128], F32).ap()
    pxw = nc.alloc_psum_tensor("p_xw", [128, 128], F32).ap()
    pv = nc.alloc_psum_tensor("p_v", [128, 128], F32).ap()
    pe = nc.alloc_psum_tensor("p_e", [128, 1024], F32).ap()
    scr = nc.alloc_psum_tensor("p_scr", [128, 512], F32).ap()
    pbt = nc.alloc_psum_tensor("p_bt", [128, 1024], BF16).ap()
    papp = nc.alloc_psum_tensor("p_app", [128, 512], F32).ap()

    cnt = {}
    semD = Sem(nc.alloc_semaphore(name="semD"), "D", cnt)
    semD0 = Sem(nc.alloc_semaphore(name="semD0"), "D0", cnt)
    semD1 = Sem(nc.alloc_semaphore(name="semD1"), "D1", cnt)
    semDr = [semD0, semD1]
    semT = Sem(nc.alloc_semaphore(name="semT"), "T", cnt)
    semV = Sem(nc.alloc_semaphore(name="semV"), "V", cnt)
    semS = Sem(nc.alloc_semaphore(name="semS"), "S", cnt)
    semG = Sem(nc.alloc_semaphore(name="semG"), "G", cnt)
    semO = Sem(nc.alloc_semaphore(name="semO"), "O", cnt)

    def emit(active):
        for k in ("D", "D0", "D1", "T", "V", "S", "G", "O"):
            cnt[k] = 0
        sy = MEng(active.get("sync"))
        t = MEng(active.get("tensor"))
        v = MEng(active.get("vector"))
        s = MEng(active.get("scalar"))
        g = MEng(active.get("gpsimd"))

        sy.dma_start(out=idn, in_=idn_h).then_inc(semD, 16)
        sy.dma_start(out=maskl, in_=maskl_h).then_inc(semD, 16)
        sy.dma_start(out=masku, in_=masku_h).then_inc(semD, 16)
        sy.dma_start(out=onesr, in_=onesr_h).then_inc(semD, 16)
        sy.dma_start(out=negonesr, in_=negonesr_h).then_inc(semD, 16)
        sy.dma_start(out=onesc, in_=onesc_h).then_inc(semD, 16)
        sy.dma_start(out=bkvr_f, in_=bkvr_h).then_inc(semD, 16)
        sy.dma_start(out=rbr_f, in_=rbr_h).then_inc(semD, 16)
        sy.dma_start(out=eta_bc, in_=eta_h.to_broadcast((128, 1))).then_inc(semD, 16)
        for db in range(8):
            sy.dma_start(out=Wf[:, db * 128:(db + 1) * 128], in_=w0t_h[db]).then_inc(semD, 16)
            sy.dma_start(out=wvf[:, db * 128:(db + 1) * 128], in_=wvt_h[db]).then_inc(semD, 16)
            sy.dma_start(out=rwf[:, db * 16:(db + 1) * 16], in_=rwt_h[db]).then_inc(semD, 16)
        for q in range(4):
            if q > 0:
                sy.wait(semV)
            sy.dma_start(out=wstg[:, 0:2048], in_=wet_h[2 * q]).then_inc(semD, 16)
            sy.dma_start(out=wstg[:, 2048:4096], in_=wet_h[2 * q + 1]).then_inc(semD, 16)
            v.wait(semD)
            v.tensor_copy(out=web[:, q * 4096:(q + 1) * 4096], in_=wstg[:, :])
            v.seal(semV)
        v.wait(semD)
        v.tensor_copy(out=Wb, in_=Wf)
        v.tensor_copy(out=wvb, in_=wvf)
        v.tensor_copy(out=rwb, in_=rwf)
        v.tensor_copy(out=bkvr, in_=bkvr_f)
        v.tensor_copy(out=rbr, in_=rbr_f)
        v.tensor_scalar(out=etan_bc, in0=eta_bc, scalar1=-CSC, scalar2=None, op0=ALU.mult)
        v.seal(semV)
        t.wait(semV)
        for db in range(8):
            t.matmul(scr[:1, 0:128], onesc, Wb[:, db * 128:(db + 1) * 128],
                     start=(db == 0), stop=(db == 7))
        for db in range(8):
            t.matmul(scr[:1, 128:256], onesc, wvb[:, db * 128:(db + 1) * 128],
                     start=(db == 0), stop=(db == 7))
        for db in range(8):
            t.matmul(scr[:1, 256:272], onesc, rwb[:, db * 16:(db + 1) * 16],
                     start=(db == 0), stop=(db == 7))
        t.seal(semT)
        v.wait(semT)
        v.tensor_copy(out=wcol0, in_=scr[0:1, 0:128])
        v.tensor_copy(out=wvcol, in_=scr[0:1, 128:256])
        v.tensor_copy(out=rwsum, in_=scr[0:1, 256:272])
        v.seal(semV)
        for q in range(4):
            t.wait(semV)
            for db in range(8):
                t.matmul(scr[:1, 0:512], onesc, web[:, db * 2048 + q * 512: db * 2048 + (q + 1) * 512],
                         start=(db == 0), stop=(db == 7))
            t.seal(semT)
            v.wait(semT)
            v.tensor_copy(out=csum_f[:, q * 512:(q + 1) * 512], in_=scr[0:1, 0:512])
            v.seal(semV)
        sy.wait(semV)
        sy.dma_start(out=csum_h, in_=csum_f[:, :].rearrange("o (e f) -> o e f", e=16)).then_inc(semD, 16)
        sy.wait(semD)
        sy.dma_start(out=csum16, in_=csum_h[0]).then_inc(semD, 16)
        v.wait(semD)
        v.tensor_copy(out=csb, in_=csum16)
        v.seal(semV)
        t.wait(semV)

        R = {}

        def seal_r(eng, sem, name, c):
            eng.seal(sem)
            R[(name, c)] = cnt[sem.key]

        def bc816(ap2d):
            return dataclasses.replace(ap2d, ap=type(ap2d.ap)([[8, 128], [1, 8], [0, 16]]))

        R[("Wb", -1)] = cnt["V"]
        R[("WbS", -1)] = cnt["S"]

        def issue_loads(c):
            if c >= nch:
                return
            sl = c % 2
            sd = semDr[sl]
            if c >= 2:
                sy.wait_at(semV, R[("bnsA", c - 2)])
                sy.wait_at(semS, R[("sexp3", c - 2)])
                sy.wait_at(semG, R[("xfb", c - 2)])
            if c >= 13:
                sy.wait_at(semV, R[("gout", c - 13)])
            sy.dma_start(out=xr_r[sl], in_=xr_h[c * 128:(c + 1) * 128, :]).then_inc(sd, 16)
            sy.dma_start(out=xf_r[sl][:, :].rearrange("p (a b) -> p a b", a=8),
                         in_=xf_h[c].transpose([1, 0, 2])).then_inc(sd, 16)
            sy.dma_start(out=xres_r[c % 13], in_=xres_h[c * 128:(c + 1) * 128, :]).then_inc(sd, 16)
            dmaD[c] = cnt[sd.key]

        def stats_prep(c):
            stats_a(c)
            stats_b(c)
            stats_c1(c)
            stats_s1a(c)
            stats_s1b(c)
            stats_s2a(c)
            stats_s2b(c)

        def stats_a(c):
            if c >= nch:
                return
            sl = c % 2
            v.wait_at(semDr[c % 2], dmaD[c])
            v.bn_stats(out=bnrec[:, 0:6], in_=xr_r[sl][:, 0:512])
            v.bn_stats(out=bnrec[:, 6:12], in_=xr_r[sl][:, 512:1024])
            seal_r(v, semV, "bnsA", c)

        def stats_b(c):
            if c >= nch:
                return
            v.wait_at(semV, R[("bnsA", c)])
            if c >= 1:
                v.wait_at(semS, R[("sexp3", c - 1)])
            v.bn_aggr(out=mv, in_=bnrec[:, :].rearrange("p (n r) -> p n r", r=3))
            seal_r(v, semV, "bnagg", c)

        def stats_c1(c):
            if c >= nch:
                return
            b2 = (c // 8) % 2
            c8 = c % 8
            sl = c % 2
            v.wait_at(semV, R[("bnagg", c)])
            if c >= 1:
                v.wait_at(semS, R[("sexp", c - 1)])
            if c >= 2:
                v.wait_at(semT, R[("transp", c - 2)])
            v.tensor_scalar(out=t0c, in0=mv[:, 1:2], scalar1=LN_EPS, scalar2=None, op0=ALU.add)
            v.tensor_scalar(out=negm8_r[b2][:, c8:c8 + 1], in0=mv[:, 0:1], scalar1=-1.0,
                            scalar2=None, op0=ALU.mult)
            v.tensor_scalar(out=m1024c, in0=mv[:, 0:1], scalar1=-1024.0, scalar2=None, op0=ALU.mult)
            v.tensor_copy(out=stats5_r[sl][:, 0:1], in_=mv[:, 0:1])
            v.tensor_scalar(out=stats5_r[sl][:, 3:4], in0=mv[:, 0:1], scalar1=-1024.0,
                            scalar2=None, op0=ALU.mult)
            v.tensor_scalar(out=stats5_r[sl][:, 4:5], in0=mv[:, 0:1], scalar1=-1.0,
                            scalar2=None, op0=ALU.mult)
            seal_r(v, semV, "bnT", c)

        def stats_s1a(c):
            if c >= nch:
                return
            b2 = (c // 8) % 2
            c8 = c % 8
            s.wait_at(semV, R[("bnT", c)])
            if c >= 2:
                s.wait_at(semV, R[("urow", c - 2)])
            if c8 == 0 and c // 8 >= 2:
                s.wait_at(semV, R[("g8", c // 8 - 2)])
            s.activation(out=lnc, in_=t0c, func=AF.Ln, bias=0.0, scale=1.0)
            s.drain()
            s.activation(out=a8_r[b2][:, c8:c8 + 1], in_=lnc, func=AF.Exp, bias=0.0, scale=-0.5)
            s.activation(out=ra_col, in_=lnc, func=AF.Exp, bias=0.0, scale=0.5)
            seal_r(s, semS, "sexpA", c)

        def stats_s1b(c):
            if c >= nch:
                return
            sl = c % 2
            b2 = (c // 8) % 2
            c8 = c % 8
            s.wait_at(semS, R[("sexpA", c)])
            if c >= 2:
                s.wait_at(semT, R[("transp", c - 2)])
            s.activation(out=stats5_r[sl][:, 1:2], in_=a8_r[b2][:, c8:c8 + 1], func=AF.Identity,
                         bias=0.0, scale=1.0)
            s.activation(out=stats5_r[sl][:, 2:3], in_=ra_col, func=AF.Identity,
                         bias=0.0, scale=1.0)
            seal_r(s, semS, "sexp", c)

        def stats_s2a(c):
            if c >= nch:
                return
            b2 = (c // 8) % 2
            c8 = c % 8
            s.wait_at(semS, R[("sexp", c)])
            s.activation(out=nca8_r[b2][:, c8:c8 + 1], in_=a8_r[b2][:, c8:c8 + 1],
                         func=AF.Identity, bias=0.0, scale=etan_bc)
            seal_r(s, semS, "sexp2", c)

        def stats_s2b(c):
            if c >= nch:
                return
            sl = c % 2
            b2 = (c // 8) % 2
            c8 = c % 8
            s.wait_at(semS, R[("sexp2", c)])
            if c >= 2:
                s.wait_at(semT, R[("gW", c - 2)])
            s.activation(out=mnca_r[sl], in_=nca8_r[b2][:, c8:c8 + 1], func=AF.Identity,
                         bias=0.0, scale=mv[:, 0:1])
            s.activation(out=xrbs_r[sl], in_=xr_r[sl], func=AF.Identity,
                         bias=0.0, scale=nca8_r[b2][:, c8:c8 + 1])
            seal_r(s, semS, "sexp3", c)

        def stats_prep(c):
            stats_a(c)
            stats_b(c)
            stats_c1(c)
            stats_s1a(c)
            stats_s1b(c)
            stats_s2a(c)
            stats_s2b(c)

        def stats_a(c):
            if c >= nch:
                return
            sl = c % 2
            v.wait_at(semDr[c % 2], dmaD[c])
            v.bn_stats(out=bnrec[:, 0:6], in_=xr_r[sl][:, 0:512])
            v.bn_stats(out=bnrec[:, 6:12], in_=xr_r[sl][:, 512:1024])
            seal_r(v, semV, "bnsA", c)

        def stats_b(c):
            if c >= nch:
                return
            v.wait_at(semV, R[("bnsA", c)])
            if c >= 1:
                v.wait_at(semS, R[("sexp3", c - 1)])
            v.bn_aggr(out=mv, in_=bnrec[:, :].rearrange("p (n r) -> p n r", r=3))
            seal_r(v, semV, "bnagg", c)

        def stats_c1(c):
            if c >= nch:
                return
            b2 = (c // 8) % 2
            c8 = c % 8
            sl = c % 2
            v.wait_at(semV, R[("bnagg", c)])
            if c >= 1:
                v.wait_at(semS, R[("sexp", c - 1)])
            if c >= 2:
                v.wait_at(semT, R[("transp", c - 2)])
            v.tensor_scalar(out=t0c, in0=mv[:, 1:2], scalar1=LN_EPS, scalar2=None, op0=ALU.add)
            v.tensor_scalar(out=negm8_r[b2][:, c8:c8 + 1], in0=mv[:, 0:1], scalar1=-1.0,
                            scalar2=None, op0=ALU.mult)
            v.tensor_scalar(out=m1024c, in0=mv[:, 0:1], scalar1=-1024.0, scalar2=None, op0=ALU.mult)
            v.tensor_copy(out=stats5_r[sl][:, 0:1], in_=mv[:, 0:1])
            v.tensor_scalar(out=stats5_r[sl][:, 3:4], in0=mv[:, 0:1], scalar1=-1024.0,
                            scalar2=None, op0=ALU.mult)
            v.tensor_scalar(out=stats5_r[sl][:, 4:5], in0=mv[:, 0:1], scalar1=-1.0,
                            scalar2=None, op0=ALU.mult)
            seal_r(v, semV, "bnT", c)

        def stats_s1a(c):
            if c >= nch:
                return
            b2 = (c // 8) % 2
            c8 = c % 8
            s.wait_at(semV, R[("bnT", c)])
            if c >= 2:
                s.wait_at(semV, R[("urow", c - 2)])
            if c8 == 0 and c // 8 >= 2:
                s.wait_at(semV, R[("g8", c // 8 - 2)])
            s.activation(out=lnc, in_=t0c, func=AF.Ln, bias=0.0, scale=1.0)
            s.drain()
            s.activation(out=a8_r[b2][:, c8:c8 + 1], in_=lnc, func=AF.Exp, bias=0.0, scale=-0.5)
            s.activation(out=ra_col, in_=lnc, func=AF.Exp, bias=0.0, scale=0.5)
            seal_r(s, semS, "sexpA", c)

        def stats_s1b(c):
            if c >= nch:
                return
            sl = c % 2
            b2 = (c // 8) % 2
            c8 = c % 8
            s.wait_at(semS, R[("sexpA", c)])
            if c >= 2:
                s.wait_at(semT, R[("transp", c - 2)])
            s.activation(out=stats5_r[sl][:, 1:2], in_=a8_r[b2][:, c8:c8 + 1], func=AF.Identity,
                         bias=0.0, scale=1.0)
            s.activation(out=stats5_r[sl][:, 2:3], in_=ra_col, func=AF.Identity,
                         bias=0.0, scale=1.0)
            seal_r(s, semS, "sexp", c)

        def stats_s2(c):
            if c >= nch:
                return
            sl = c % 2
            b2 = (c // 8) % 2
            c8 = c % 8
            s.wait_at(semS, R[("sexp", c)])
            if c >= 2:
                s.wait_at(semT, R[("gW", c - 2)])
            s.activation(out=nca8_r[b2][:, c8:c8 + 1], in_=a8_r[b2][:, c8:c8 + 1],
                         func=AF.Identity, bias=0.0, scale=etan_bc)
            s.drain()
            s.activation(out=mnca_r[sl], in_=nca8_r[b2][:, c8:c8 + 1], func=AF.Identity,
                         bias=0.0, scale=mv[:, 0:1])
            s.activation(out=xrbs_r[sl], in_=xr_r[sl], func=AF.Identity,
                         bias=0.0, scale=nca8_r[b2][:, c8:c8 + 1])
            seal_r(s, semS, "sexp2", c)

        dmaD = {}
        issue_loads(0)
        g.wait_at(semDr[0], dmaD[0])
        g.tensor_copy(out=xfb_r[0], in_=xf_r[0])
        seal_r(g, semG, "xfb", 0)
        stats_prep(0)
        t.wait_at(semG, R[("xfb", 0)])
        for db in range(8):
            xb0 = xfb_r[0][:, db * 128:(db + 1) * 128]
            t.matmul(pgram, xb0, xb0, start=(db == 0), stop=False)
        for db in range(8):
            xb0 = xfb_r[0][:, db * 128:(db + 1) * 128]
            t.matmul(pv, xb0, wvb[:, db * 128:(db + 1) * 128], start=(db == 0), stop=False)
        for db in range(8):
            xb0 = xfb_r[0][:, db * 128:(db + 1) * 128]
            t.matmul(scr[:, OFF_LOG:OFF_LOG + 16], xb0, rwb[:, db * 16:(db + 1) * 16],
                     start=(db == 0), stop=False)
        seal_r(t, semT, "sweepbig", 0)
        t.wait_at(semS, R[("sexp", 0)])
        for k in range(5):
            t.transpose(pbt[0:1, k * 128:(k + 1) * 128], stats5_r[0][:, k:k + 1], idn)
        seal_r(t, semT, "transp", 0)

        for ch in range(nch + LAG):
            act = ch < nch
            lc = ch - LAG
            sl = ch % 2
            b2 = (ch // 8) % 2
            c8 = ch % 8
            xfb = xfb_r[ch % 12]
            acol = a8_r[b2][:, c8:c8 + 1]
            ncacol = nca8_r[b2][:, c8:c8 + 1]
            if lc >= 0:
                lbk = lc // 8
                lgate = gate8_r[lbk % 3]
                lgcol = lambda e: lgate[:, (lc % 8) * 16 + e:(lc % 8) * 16 + e + 1]

            if act:
                issue_loads(ch + 1)

            if c8 == 0 and ch // 8 >= 1 and (ch // 8 - 1) < nch // 8:
                bk = ch // 8 - 1
                eL8 = eL8_r[bk % 2]
                if bk >= 3:
                    v.wait_at(semT, R[("term2", 8 * (bk - 2) - 1)])
                    v.wait_at(semS, R[("evacBs", 8 * (bk - 2) - 1)])
                v.wait_at(semS, R[("eL", 8 * bk + 7)])
                for k in range(8):
                    v.max(out=m88[:, k * 8:(k + 1) * 8], in_=eL8[:, k * 16:(k + 1) * 16])
                v.tensor_scalar(out=epsZ8, in0=se8_r[bk % 2], scalar1=EPS, scalar2=None, op0=ALU.mult)
                v.drain()
                v.reduce_sum(out=S68, in_=m88[:, :].rearrange("p (k e) -> p k e", k=8)[:, :, 0:6],
                             axis=mybir.AxisListType.X)
                m5 = dataclasses.replace(m88[:, :], offset=m88.offset + 5,
                                         ap=type(m88.ap)([[64, 128], [8, 8], [0, 16]]))
                v.tensor_tensor(out=maskt8, in0=eL8[:, :].rearrange("p (k e) -> p k e", k=8),
                                in1=m5, op=ALU.is_ge)
                v.drain()
                v.reciprocal(out=rS28, in_=S68)
                v.tensor_tensor(out=den8, in0=S68, in1=epsZ8, op=ALU.add)
                v.tensor_tensor(out=graw8, in0=maskt8, in1=eL8, op=ALU.mult)
                v.drain()
                v.tensor_tensor(out=gateA8, in0=graw8[:, :].rearrange("p (k e) -> p k e", k=8),
                                in1=bc816(rS28[:, :]), op=ALU.mult)
                v.reciprocal(out=rden8, in_=den8)
                v.drain()
                v.tensor_tensor(out=gate8_r[bk % 3],
                                in0=gateA8[:, :].rearrange("p (k e) -> p k e", k=8),
                                in1=bc816(a8_r[bk % 2][:, :]), op=ALU.mult)
                v.tensor_tensor(out=sig8, in0=S68, in1=rden8, op=ALU.mult)
                v.drain()
                v.tensor_tensor(out=gm8_r[bk % 3],
                                in0=gate8_r[bk % 3][:, :].rearrange("p (k e) -> p k e", k=8),
                                in1=bc816(negm8_r[bk % 2][:, :]), op=ALU.mult)
                v.tensor_scalar(out=epssig8_r[bk % 3], in0=sig8, scalar1=EPS, scalar2=None, op0=ALU.mult)
                seal_r(v, semV, "g8", bk)

            if act:
                s.wait_at(semT, R[("transp", ch)])
                for k in range(5):
                    s.activation(out=rows_r[sl][k], in_=pbt[0:1, k * 128:(k + 1) * 128],
                                 func=AF.Identity, bias=0.0, scale=1.0)
                seal_r(s, semS, "rows", ch)

            if act and lc >= 1:
                s.wait_at(semV, R[("moe3", lc - 1)])
                s.activation(out=eSg, in_=moe3, func=AF.Exp, bias=0.0, scale=-1.0)
                seal_r(s, semS, "eSg", lc - 1)
                if lc >= 2:
                    s.wait_at(semG, R[("ghs", lc - 2)])
                s.wait_at(semS, R[("eSg", lc - 1)])
                s.activation(out=lnd, in_=eSg, func=AF.Ln, bias=1.0, scale=1.0)
                s.drain()
                s.activation(out=rdsg, in_=lnd, func=AF.Exp, bias=0.0, scale=-1.0)
                seal_r(s, semS, "dsg", lc - 1)

            if act and lc >= 1:
                stats_s2b(ch)

            if act:
                rows = rows_r[sl]
                t.wait_at(semS, R[("rows", ch)])
                t.matmul(scr[:, OFF_LOG:OFF_LOG + 16], rows[4], rwsum, start=False, stop=False)
                t.matmul(scr[:, OFF_LOG:OFF_LOG + 16], rows[2], rbr, start=False, stop=True)
                t.matmul(scr[:, OFF_ABC:OFF_ABC + 128], onesr, rows[1], start=True, stop=True)
                seal_r(t, semT, "abc", ch)
                t.matmul(pgram, rows[3], rows[0], start=False, stop=True)
                t.matmul(pv, rows[4], wvcol, start=False, stop=False)
                t.matmul(pv, rows[2], bkvr, start=False, stop=True)
                if ch >= 1:
                    t.wait_at(semV, R[("H", ch - 1)])
                t.wait_at(semS, R[("WbS", ch - 1)])
                for db in range(8):
                    xb = xfb[:, db * 128:(db + 1) * 128]
                    t.matmul(pxw, xb, Wb[:, db * 128:(db + 1) * 128], start=(db == 0), stop=False)
                t.matmul(pxw, rows[4], wcol0, start=False, stop=True)
                seal_r(t, semT, "k1s", ch)
                if lc >= 0:
                    lfb0 = xfb_r[lc % 12]
                    t.wait_at(semV, R[("g8", lc // 8)])
                    for db in range(4):
                        xb = lfb0[:, db * 128:(db + 1) * 128]
                        base = db * 2048
                        t.matmul(pe[:, 0:512], xb, web[:, base:base + 512],
                                 start=(db == 0), stop=False)
                        t.matmul(pe[:, 512:1024], xb, web[:, base + 512:base + 1024],
                                 start=(db == 0), stop=False)

            if act:
                v.wait_at(semT, R[("abc", ch)])
                v.tensor_tensor(out=amaskl_r[sl], in0=scr[:, OFF_ABC:OFF_ABC + 128],
                                in1=maskl, op=ALU.mult)
                v.tensor_tensor(out=amasku_r[sl], in0=scr[:, OFF_ABC:OFF_ABC + 128],
                                in1=masku, op=ALU.mult)
                seal_r(v, semV, "amask", ch)
                v.wait_at(semT, R[("k1s", ch)])
                v.tensor_scalar(out=Vn, in0=pv, scalar1=acol, scalar2=None, op0=ALU.mult)
                seal_r(v, semV, "vn", ch)
                v.wait_at(semV, R[("amask", ch)])
                v.scalar_tensor_tensor(out=Np_b[0], in0=pgram, scalar=ncacol, in1=amaskl_r[sl],
                                       op0=ALU.mult, op1=ALU.mult)
                v.scalar_tensor_tensor(out=Nt_b[0], in0=pgram, scalar=ncacol, in1=amasku_r[sl],
                                       op0=ALU.mult, op1=ALU.mult)
                v.wait_at(semV, R[("vn", ch)])
                v.scalar_tensor_tensor(out=Ef[0], in0=pxw, scalar=acol, in1=Vn,
                                       op0=ALU.mult, op1=ALU.subtract)
                v.scalar_tensor_tensor(out=Eb[0], in0=pxw, scalar=acol, in1=Vn,
                                       op0=ALU.mult, op1=ALU.subtract)
                seal_r(v, semV, "H", ch)

            if act:
                s.wait_at(semT, R[("k1s", ch)])
                s.wait_at(semV, R[("amask", ch)])
                if c8 == 0 and ch // 8 >= 2:
                    s.wait_at(semV, R[("g8", ch // 8 - 2)])
                s.activation(out=eL8_r[b2][:, c8 * 16:(c8 + 1) * 16],
                             in_=scr[:, OFF_LOG:OFF_LOG + 16], func=AF.Exp,
                             bias=0.0, scale=acol, accum_out=se8_r[b2][:, c8:c8 + 1])
                seal_r(s, semS, "eL", ch)

            if act:
                for j in range(4):
                    cur, nxt = j % 2, (j + 1) % 2
                    t.wait_at(semV, R[("H", ch)] if j == 0 else R[(f"stt{j-1}", ch)])
                    t.matmul(papp[:, OFF_APP:OFF_APP + 128], Nt_b[cur], Eb[cur],
                             start=True, stop=True)
                    if j < 3:
                        t.matmul(pgram, Nt_b[cur], Np_b[cur], start=True, stop=True)
                        t.matmul(pv, Np_b[cur], Nt_b[cur], start=True, stop=True)
                    seal_r(t, semT, f"app{j}", ch)
                    if lc >= 0 and j == 0:
                        lfb = xfb_r[lc % 12]
                        for db in range(4, 8):
                            xb = lfb[:, db * 128:(db + 1) * 128]
                            base = db * 2048
                            t.matmul(pe[:, 0:512], xb, web[:, base:base + 512],
                                     start=False, stop=(db == 7))
                            t.matmul(pe[:, 512:1024], xb, web[:, base + 512:base + 1024],
                                     start=False, stop=(db == 7))
                        seal_r(t, semT, "grp0", lc)
                    if lc >= 0 and j == 2:
                        t.wait_at(semS, R[("rows", ch)])
                        t.transpose(pbt[0:16, 640:768],
                                    gm8_r[(lc // 8) % 3][:, (lc % 8) * 16:(lc % 8 + 1) * 16], idn)
                        seal_r(t, semT, "gmT", lc)
                    if lc >= 0 and j in (2, 3):
                        lfb = xfb_r[lc % 12]
                        if j == 2:
                            t.wait_at(semV, R[("evacA", lc)])
                            t.wait_at(semS, R[("evacAs", lc)])
                        for db in range(4 * (j - 2), 4 * (j - 2) + 4):
                            xb = lfb[:, db * 128:(db + 1) * 128]
                            base = db * 2048 + 1024
                            t.matmul(pe[:, 0:512], xb, web[:, base:base + 512],
                                     start=(db == 0), stop=(db == 7))
                            t.matmul(pe[:, 512:1024], xb, web[:, base + 512:base + 1024],
                                     start=(db == 0), stop=(db == 7))
                        if j == 3:
                            seal_r(t, semT, "grp1", lc)
                    v.wait_at(semT, R[(f"app{j}", ch)])
                    v.scalar_tensor_tensor(out=Ef[nxt], in0=papp[:, OFF_APP:OFF_APP + 128],
                                           scalar=1.0, in1=Ef[cur], op0=ALU.mult, op1=ALU.add)
                    v.scalar_tensor_tensor(out=Eb[nxt], in0=papp[:, OFF_APP:OFF_APP + 128],
                                           scalar=1.0, in1=Ef[cur], op0=ALU.mult, op1=ALU.add)
                    if j < 3:
                        v.tensor_copy(out=Np_b[nxt], in_=pgram)
                        v.tensor_copy(out=Nt_b[nxt], in_=pv)
                    if lc >= 0 and j == 2:
                        v.wait_at(semT, R[("gmT", lc)])
                        v.tensor_copy(out=gmT_r[lc % 12], in_=pbt[0:16, 640:768])
                    if lc >= 0 and j == 3:
                        v.wait_at(semT, R[("grp1", lc)])
                        for k in range(4):
                            v.tensor_scalar(out=tmp16[:, k * 128:(k + 1) * 128],
                                            in0=pe[:, k * 128:(k + 1) * 128],
                                            scalar1=lgcol(8 + k), scalar2=None, op0=ALU.mult)
                    seal_r(v, semV, f"stt{j}", ch)
                    if lc >= 1 and j == 0:
                        v.wait_at(semS, R[("dsg", lc - 1)])
                        if lc >= 2:
                            v.wait_at(semG, R[("ghs", lc - 2)])
                        v.tensor_tensor(out=h_t, in0=s_tile_r[(lc - 1) % 12], in1=moe3,
                                        op=ALU.mult)
                        seal_r(v, semV, "hrdsg", lc - 1)
                    if lc >= 0 and j == 1:
                        v.wait_at(semT, R[("grp0", lc)])
                        if lc >= 1:
                            v.wait_at(semV, R[("gpred", lc - 1)])
                        for k in range(4):
                            v.tensor_scalar(out=tmp16[:, (k + 8) * 128:(k + 9) * 128],
                                            in0=pe[:, k * 128:(k + 1) * 128],
                                            scalar1=lgcol(k), scalar2=None, op0=ALU.mult)
                        seal_r(v, semV, "evacA", lc)
                        s.wait_at(semT, R[("grp0", lc)])
                        if lc >= 1:
                            s.wait_at(semV, R[("gpred", lc - 1)])
                        for k in range(4, 8):
                            s.activation(out=tmp16[:, (k + 8) * 128:(k + 9) * 128],
                                         in_=pe[:, k * 128:(k + 1) * 128], func=AF.Identity,
                                         bias=0.0, scale=lgcol(k))
                        seal_r(s, semS, "evacAs", lc)
                    if lc >= 1 and j == 1:
                        g.wait_at(semV, R[("hrdsg", lc - 1)])
                        g.tensor_tensor(out=hs, in0=h_t, in1=rdsg, op=ALU.mult)
                        seal_r(g, semG, "ghs", lc - 1)
                        v.wait_at(semG, R[("ghs", lc - 1)])
                        if lc >= 2:
                            v.wait_at(semO, 16 * (lc - 1))
                        v.tensor_tensor(out=out_t, in0=hs, in1=xres_r[(lc - 1) % 13],
                                        op=ALU.add)
                        seal_r(v, semV, "gout", lc - 1)
                        sy.wait_at(semV, R[("gout", lc - 1)])
                        sy.dma_start(out=out_h[(lc - 1) * 128:lc * 128, :],
                                     in_=out_t).then_inc(semO, 16)
                    if lc >= 0 and j == 2:
                        R[("gmTe_l", lc)] = R[("stt2", ch)]
                    if lc >= 0 and j == 3:
                        R[("evacB", lc)] = R[("stt3", ch)]
                R[("gmTe", ch)] = R[("stt2", ch)]
            else:
                s.wait_at(semV, R[("moe3", lc - 1)])
                s.activation(out=eSg, in_=moe3, func=AF.Exp, bias=0.0, scale=-1.0)
                seal_r(s, semS, "eSg", lc - 1)
                s.wait_at(semG, R[("ghs", lc - 2)])
                s.wait_at(semS, R[("eSg", lc - 1)])
                s.activation(out=lnd, in_=eSg, func=AF.Ln, bias=1.0, scale=1.0)
                s.drain()
                s.activation(out=rdsg, in_=lnd, func=AF.Exp, bias=0.0, scale=-1.0)
                seal_r(s, semS, "dsg", lc - 1)
                v.wait_at(semS, R[("dsg", lc - 1)])
                v.wait_at(semG, R[("ghs", lc - 2)])
                v.tensor_tensor(out=h_t, in0=s_tile_r[(lc - 1) % 12], in1=moe3, op=ALU.mult)
                seal_r(v, semV, "hrdsg", lc - 1)
                g.wait_at(semV, R[("hrdsg", lc - 1)])
                g.tensor_tensor(out=hs, in0=h_t, in1=rdsg, op=ALU.mult)
                seal_r(g, semG, "ghs", lc - 1)
                v.wait_at(semG, R[("ghs", lc - 1)])
                v.wait_at(semO, 16 * (lc - 1))
                v.tensor_tensor(out=out_t, in0=hs, in1=xres_r[(lc - 1) % 13], op=ALU.add)
                seal_r(v, semV, "gout", lc - 1)
                sy.wait_at(semV, R[("gout", lc - 1)])
                sy.dma_start(out=out_h[(lc - 1) * 128:lc * 128, :],
                             in_=out_t).then_inc(semO, 16)
                lfb = xfb_r[lc % 12]
                t.wait_at(semV, R[("g8", lc // 8)])
                t.wait_at(semV, R[("Wf", min(ch - 1, nch - 1))])
                t.wait_at(semV, R[("evacB", lc - 1)])
                t.wait_at(semS, R[("evacBs", lc - 1)])
                for db in range(8):
                    xb = lfb[:, db * 128:(db + 1) * 128]
                    base = db * 2048
                    t.matmul(pe[:, 0:512], xb, web[:, base:base + 512],
                             start=(db == 0), stop=(db == 7))
                    t.matmul(pe[:, 512:1024], xb, web[:, base + 512:base + 1024],
                             start=(db == 0), stop=(db == 7))
                seal_r(t, semT, "grp0", lc)
                v.wait_at(semT, R[("grp0", lc)])
                v.wait_at(semV, R[("gpred", lc - 1)])
                for k in range(4):
                    v.tensor_scalar(out=tmp16[:, (k + 8) * 128:(k + 9) * 128],
                                    in0=pe[:, k * 128:(k + 1) * 128],
                                    scalar1=lgcol(k), scalar2=None, op0=ALU.mult)
                seal_r(v, semV, "evacA", lc)
                s.wait_at(semT, R[("grp0", lc)])
                s.wait_at(semV, R[("gpred", lc - 1)])
                for k in range(4, 8):
                    s.activation(out=tmp16[:, (k + 8) * 128:(k + 9) * 128],
                                 in_=pe[:, k * 128:(k + 1) * 128], func=AF.Identity,
                                 bias=0.0, scale=lgcol(k))
                seal_r(s, semS, "evacAs", lc)
                t.wait_at(semV, R[("evacA", lc)])
                t.wait_at(semS, R[("evacAs", lc)])
                for db in range(8):
                    xb = lfb[:, db * 128:(db + 1) * 128]
                    base = db * 2048 + 1024
                    t.matmul(pe[:, 0:512], xb, web[:, base:base + 512],
                             start=(db == 0), stop=(db == 7))
                    t.matmul(pe[:, 512:1024], xb, web[:, base + 512:base + 1024],
                             start=(db == 0), stop=(db == 7))
                seal_r(t, semT, "grp1", lc)
                if lc >= 1:
                    t.wait_at(semV, R[("gmTe_l", lc - 1)])
                t.transpose(pbt[0:16, 640:768],
                            gm8_r[(lc // 8) % 3][:, (lc % 8) * 16:(lc % 8 + 1) * 16], idn)
                seal_r(t, semT, "gmT", lc)
                v.wait_at(semT, R[("gmT", lc)])
                v.tensor_copy(out=gmT_r[lc % 12], in_=pbt[0:16, 640:768])
                seal_r(v, semV, "gmTe_l", lc)
                v.wait_at(semT, R[("grp1", lc)])
                for k in range(4):
                    v.tensor_scalar(out=tmp16[:, k * 128:(k + 1) * 128],
                                    in0=pe[:, k * 128:(k + 1) * 128],
                                    scalar1=lgcol(8 + k), scalar2=None, op0=ALU.mult)
                seal_r(v, semV, "evacB", lc)

            if act and ch + 1 < nch:
                g.wait_at(semDr[(ch + 1) % 2], dmaD[ch + 1])
                if ch >= 11:
                    g.wait_at(semT, R[("grp1", ch - 11)])
                g.tensor_copy(out=xfb_r[(ch + 1) % 12], in_=xf_r[(ch + 1) % 2])
                seal_r(g, semG, "xfb", ch + 1)

            if lc >= 0:
                s.wait_at(semT, R[("grp1", lc)])
                for k in range(4, 8):
                    s.activation(out=tmp16[:, k * 128:(k + 1) * 128],
                                 in_=pe[:, k * 128:(k + 1) * 128], func=AF.Identity,
                                 bias=0.0, scale=lgcol(8 + k))
                seal_r(s, semS, "evacBs", lc)
            if lc >= 0 and not act:
                pass

            if act:
                t.wait_at(semV, R[("stt3", ch)])
                t.wait_at(semS, R[("sexp3", ch)])
                t.matmul(papp[:1, OFF_U:OFF_U + 128], mnca_r[sl], Eb[0], start=True, stop=True)
                seal_r(t, semT, "umm", ch)
                v.wait_at(semT, R[("umm", ch)])
                v.tensor_copy(out=u_row, in_=papp[0:1, OFF_U:OFF_U + 128])
                v.tensor_tensor(out=s_tile_r[ch % 12], in0=Ef[0], in1=Vn, op=ALU.add)
                seal_r(v, semV, "urow", ch)
                if lc >= 0:
                    stats_a(ch + 1)
                t.wait_at(semV, R[("urow", ch)])
                if lc >= 0:
                    t.wait_at(semS, R[("evacBs", lc)])
                for db in range(8):
                    t.matmul(pe[:, db * 128:(db + 1) * 128],
                             xrbs_r[sl][:, db * 128:(db + 1) * 128], Eb[0],
                             start=(db % 4 == 0), stop=False)
                ur_bc = dataclasses.replace(u_row, ap=type(u_row.ap)([[128, 1], [0, 4], [1, 128]]))
                t.matmul(pe[:, 0:512], negonesr, ur_bc, start=False, stop=True)
                t.matmul(pe[:, 512:1024], negonesr, ur_bc, start=False, stop=True)
                seal_r(t, semT, "gW", ch)
                v.wait_at(semT, R[("gW", ch)])
                v.scalar_tensor_tensor(out=Wf, in0=pe, scalar=1.0, in1=Wf,
                                       op0=ALU.mult, op1=ALU.add)
                seal_r(v, semV, "Wf", ch)
                if lc >= 0:
                    stats_b(ch + 1)
                    stats_c1(ch + 1)
                s.wait_at(semV, R[("Wf", ch)])
                s.activation(out=Wb, in_=Wf, func=AF.Identity, bias=0.0, scale=1.0)
                seal_r(s, semS, "WbS", ch)
                if lc >= 0:
                    stats_s1a(ch + 1)
                    stats_s1b(ch + 1)
                    stats_s2a(ch + 1)

            if lc >= 0:
                if act:
                    t.wait_at(semV, R[("stt2", ch)])
                elif lc >= 1:
                    t.wait_at(semV, R[("moe3", lc - 1)])
                t.wait_at(semV, R[("gmTe_l", lc)])
                t.matmul(papp[:, OFF_T2:OFF_T2 + 128], gmT_r[lc % 12], csb, start=True, stop=True)
                seal_r(t, semT, "term2", lc)
                v.wait_at(semS, R[("evacBs", lc)])
                v.wait_at(semV, R[("evacB", lc)])
                v.reduce_sum(out=moe, in_=tmp16[:, :].rearrange("p (e f) -> p f e", e=16),
                             axis=mybir.AxisListType.X)
                seal_r(v, semV, "gpred", lc)
                v.wait_at(semT, R[("term2", lc)])
                if act:
                    v.wait_at(semT, R[("umm", ch)])
                if lc >= 1:
                    v.wait_at(semS, R[("eSg", lc - 1)])
                v.wait_at(semV, R[("gpred", lc)])
                v.scalar_tensor_tensor(out=moe3, in0=papp[:, OFF_T2:OFF_T2 + 128],
                                       scalar=epssig8_r[(lc // 8) % 3][:, lc % 8:lc % 8 + 1],
                                       in1=moe, op0=ALU.add, op1=ALU.add)
                seal_r(v, semV, "moe3", lc)

            if act and lc < 0:
                stats_prep(ch + 1)

            if ch + 1 < nch:
                nsl = (ch + 1) % 2
                nfb = xfb_r[(ch + 1) % 12]
                t.wait_at(semV, R[("stt2", ch)])
                t.wait_at(semS, R[("eL", ch)])
                t.wait_at(semG, R[("xfb", ch + 1)])
                for db in range(8):
                    xb = nfb[:, db * 128:(db + 1) * 128]
                    t.matmul(pgram, xb, xb, start=(db == 0), stop=False)
                for db in range(8):
                    xb = nfb[:, db * 128:(db + 1) * 128]
                    t.matmul(pv, xb, wvb[:, db * 128:(db + 1) * 128], start=(db == 0), stop=False)
                for db in range(8):
                    xb = nfb[:, db * 128:(db + 1) * 128]
                    t.matmul(scr[:, OFF_LOG:OFF_LOG + 16], xb, rwb[:, db * 16:(db + 1) * 16],
                             start=(db == 0), stop=False)
                seal_r(t, semT, "sweepbig", ch + 1)
                t.wait_at(semS, R[("sexp", ch + 1)])
                t.wait_at(semV, R[("gmTe", ch)])
                for k in range(5):
                    t.transpose(pbt[0:1, k * 128:(k + 1) * 128], stats5_r[nsl][:, k:k + 1], idn)
                seal_r(t, semT, "transp", ch + 1)

        fl = nch - 1
        s.wait_at(semV, R[("moe3", fl)])
        s.activation(out=eSg, in_=moe3, func=AF.Exp, bias=0.0, scale=-1.0)
        seal_r(s, semS, "eSg", fl)
        s.wait_at(semG, R[("ghs", fl - 1)])
        s.wait_at(semS, R[("eSg", fl)])
        s.activation(out=lnd, in_=eSg, func=AF.Ln, bias=1.0, scale=1.0)
        s.drain()
        s.activation(out=rdsg, in_=lnd, func=AF.Exp, bias=0.0, scale=-1.0)
        seal_r(s, semS, "dsg", fl)
        v.wait_at(semS, R[("dsg", fl)])
        v.wait_at(semG, R[("ghs", fl - 1)])
        v.tensor_tensor(out=h_t, in0=s_tile_r[fl % 12], in1=moe3, op=ALU.mult)
        seal_r(v, semV, "hrdsg", fl)
        g.wait_at(semV, R[("hrdsg", fl)])
        g.tensor_tensor(out=hs, in0=h_t, in1=rdsg, op=ALU.mult)
        seal_r(g, semG, "ghs", fl)
        v.wait_at(semG, R[("ghs", fl)])
        v.wait_at(semO, 16 * fl)
        v.tensor_tensor(out=out_t, in0=hs, in1=xres_r[fl % 13], op=ALU.add)
        seal_r(v, semV, "gout", fl)
        sy.wait_at(semV, R[("gout", fl)])
        sy.dma_start(out=out_h[fl * 128:(fl + 1) * 128, :], in_=out_t).then_inc(semO, 16)

        sy.wait(semO)

    import os as _os
    _snap = {}
    def _chk(tag):
        import copy as _copy
        if _os.environ.get("KDBG_CNT"):
            if _snap:
                k0 = next(iter(_snap))
                if _snap[k0] != cnt:
                    print("COUNTER MISMATCH", k0, "vs", tag)
                    for kk in cnt:
                        if _snap[k0][kk] != cnt[kk]:
                            print("  ", kk, _snap[k0][kk], cnt[kk])
            _snap[tag] = dict(cnt)

    from contextlib import ExitStack
    with nc.Block() as blk:
        @blk.sync
        def _(eng):
            emit({"sync": eng})
            _chk("sync")

        @blk.vector
        def _(eng):
            emit({"vector": eng})
            _chk("vector")

        @blk.tensor
        def _(eng):
            emit({"tensor": eng})
            _chk("tensor")

        @blk.scalar
        def _(eng):
            emit({"scalar": eng})
            _chk("scalar")

        @blk.gpsimd
        def _(eng):
            emit({"gpsimd": eng})
            _chk("gpsimd")

    return nc


def _host_inputs(L0_attn_x, L0_attn_W0, L0_attn_Wkv, L0_attn_bkv, L0_attn_eta,
                 L0_ffn_router_w, L0_ffn_router_b, L0_ffn_expert_w):
    f = np.float32
    x = np.ascontiguousarray(np.asarray(L0_attn_x, f))
    W0 = np.asarray(L0_attn_W0, f)
    Wkv = np.asarray(L0_attn_Wkv, f)
    bkv = np.asarray(L0_attn_bkv, f)
    eta = np.asarray(L0_attn_eta, f).reshape(1, 1)
    rw = np.asarray(L0_ffn_router_w, f)
    rb = np.asarray(L0_ffn_router_b, f)
    We = np.asarray(L0_ffn_expert_w, f)

    xr_h = np.ascontiguousarray(x.transpose(1, 0, 2).reshape(T, D))
    xf_flat = np.ascontiguousarray(xr_h.T)
    xf_h = np.ascontiguousarray(
        xf_flat.reshape(8, 128, N_CH, 128).transpose(2, 0, 1, 3))

    tstep = np.arange(128) // B
    maskl = (tstep[:, None] > tstep[None, :]).astype(ml_dtypes.bfloat16)
    masku = (tstep[:, None] < tstep[None, :]).astype(ml_dtypes.bfloat16)
    idn = np.eye(128, dtype=f).astype(ml_dtypes.bfloat16)
    onesr = np.ones((1, 128), f).astype(ml_dtypes.bfloat16)
    negonesr = (-np.ones((1, 128), f)).astype(ml_dtypes.bfloat16)
    onesc = np.ones((128, 1), f).astype(ml_dtypes.bfloat16)

    in_maps = []
    for j in range(NCORES):
        fs = slice(j * FSL, (j + 1) * FSL)
        w0t = np.ascontiguousarray(W0[fs, :].T.reshape(8, 128, 128))
        wvt = np.ascontiguousarray(Wkv[D + j * FSL:D + (j + 1) * FSL, :].T.reshape(8, 128, 128))
        bkvr = np.ascontiguousarray(bkv[D + j * FSL:D + (j + 1) * FSL].reshape(1, 128))
        rwt = np.ascontiguousarray(rw.T.reshape(8, 128, 16))
        rbr = np.ascontiguousarray(rb.reshape(1, 16))
        wet = np.ascontiguousarray(
            We[:, :, fs].reshape(16, 8, 128, 128).transpose(1, 2, 0, 3).reshape(8, 128, 2048))
        xres = np.ascontiguousarray(xr_h[:, fs])
        in_maps.append({
            "xr": xr_h, "xf": xf_h, "xres": xres,
            "w0t": w0t, "wvt": wvt, "bkvr": bkvr, "eta": eta,
            "rwt": rwt, "rbr": rbr, "wet": wet,
            "idn": idn, "maskl": maskl, "masku": masku,
            "onesr": onesr, "negonesr": negonesr, "onesc": onesc,
        })
    return in_maps


def kernel(L0_attn_x, L0_attn_W0, L0_attn_Wkv, L0_attn_bkv, L0_attn_eta,
           L0_ffn_router_w, L0_ffn_router_b, L0_ffn_expert_w):
    in_maps = _host_inputs(L0_attn_x, L0_attn_W0, L0_attn_Wkv, L0_attn_bkv,
                           L0_attn_eta, L0_ffn_router_w, L0_ffn_router_b,
                           L0_ffn_expert_w)
    nc = build_program()
    res = run_bass_kernel_spmd(nc, in_maps, core_ids=list(range(NCORES)))
    full = np.concatenate([res.results[j]["out"] for j in range(NCORES)], axis=1)
    return np.ascontiguousarray(full.reshape(S, B, D).transpose(1, 0, 2))
